# version 05
# baseline (speedup 1.0000x reference)
import contextlib
import os
import numpy as np
import ml_dtypes
import concourse.bass as bass
import concourse.mybir as mybir
from concourse.bass_utils import run_bass_kernel_spmd

F32 = mybir.dt.float32
BF16 = mybir.dt.bfloat16
OP = mybir.AluOpType
ACT = mybir.ActivationFunctionType

NT = 2048
SP = 2064
NF = 8
NKC = 17
NH = 16
ENGS = ("sync", "gpsimd", "tensor", "scalar", "vector")

_UID = [0]


def uid():
    _UID[0] += 1
    return _UID[0]


class Ev:
    __slots__ = ("eng", "waits", "fn", "incs")

    def __init__(self, eng, waits, fn, incs):
        self.eng, self.waits, self.fn, self.incs = eng, waits, fn, incs


class Builder:
    def __init__(self, nc, sems):
        self.nc = nc
        self.sems = sems
        self.C = {k: 0 for k in sems}
        self.EV = []
        self.nbar = 0

    def ev(self, eng, fn, waits=(), incs=()):
        selfsem = {"tensor": "tM", "vector": "vV", "scalar": "sS"}.get(eng)
        waits = tuple((s, v) for (s, v) in waits if s != selfsem and v > 0)
        e = Ev(eng, waits, fn, tuple(incs))
        self.EV.append(e)
        for s, k in incs:
            self.C[s] += k
        if fn is not None and eng in ("tensor", "scalar", "vector"):
            if not hasattr(self, "last_real"):
                self.last_real = {}
            self.last_real[eng] = e

    def barrier(self):
        own = {"tensor": "tM", "scalar": "sS", "vector": "vV"}
        lr = getattr(self, "last_real", {})
        for e, sem in own.items():
            ev2 = lr.get(e)
            if ev2 is not None and not ev2.incs:
                ev2.incs = ((sem, 1),)
                self.C[sem] += 1
        snap = dict(self.C)
        for e in ENGS:
            waits = tuple((s, snap[s]) for s in ("dG", "dS", "tM", "vV", "sS"))
            self.ev(e, None, waits=waits)
        self.last_real = {}

    def emit(self, blk):
        sems = self.sems
        for eng_name in ENGS:
            evs = [e for e in self.EV if e.eng == eng_name]

            def body(eng, evs=evs):
                last = {}
                for ev in evs:
                    for (s, v) in ev.waits:
                        if last.get(s, -1) < v:
                            eng.wait_ge(sems[s], v)
                            last[s] = v
                    if ev.fn is None:
                        ins = eng.nop()
                    else:
                        ins = ev.fn(eng)
                    for (s, k) in ev.incs:
                        ins.then_inc(sems[s], k)

            getattr(blk, eng_name)(body)


def build_program():
    nc = bass.Bass("TRN2", target_bir_lowering=False, detect_race_conditions=False)

    d = {}

    def din(name, shape, dt=F32):
        d[name] = nc.dram_tensor(name, shape, dt, kind="ExternalInput").ap()
    din("x", [1024, NT])
    d["y"] = nc.dram_tensor("y", [1024, NT], F32, kind="ExternalOutput").ap()
    for n in ("awq", "awqc", "awk", "awkc"):
        din(n, [1024, 512])
    for n in ("awv", "awvc"):
        din(n, [1024, 1024])
    for n in ("abq", "abqc", "abk", "abkc"):
        din(n, [1, 512])
    for n in ("abv", "abvc"):
        din(n, [1, 1024])
    din("bwin", [1024, 1024])
    din("bwout", [1024, 1024])
    din("bwgate", [1024, 64])
    din("bbgate", [1, 64])
    din("beig", [16, 1])
    din("bgam", [128, 8])
    din("bbet", [128, 8])
    din("inda", [16, 1024])
    din("indb", [48, 1024])
    din("fwg", [8, 1024, 4096])
    din("fwu", [8, 1024, 4096])
    din("fwd", [8, 8, 32, 128, 128])

    stack = contextlib.ExitStack()
    E = stack.enter_context

    sems = {}
    for s in ("PH", "dG", "dS", "tM", "vV", "sS"):
        sems[s] = E(nc.semaphore(s))
    B = Builder(nc, sems)
    C = B.C

    XS = E(nc.sbuf_tensor("XS", [128, NF * NT], F32))
    ones_bf = E(nc.sbuf_tensor("ones_bf", [128, 1], BF16))
    onesr_bf = E(nc.sbuf_tensor("onesr_bf", [1, 512], BF16))
    onesb_f = E(nc.sbuf_tensor("onesb_f", [1, 128], F32))
    s_gam = E(nc.sbuf_tensor("s_gam", [128, 8], F32))
    s_bet = E(nc.sbuf_tensor("s_bet", [128, 8], F32))
    s_eig = E(nc.sbuf_tensor("s_eig", [16, 1], F32))
    s_eigr = E(nc.sbuf_tensor("s_eigr", [16, 1], F32))
    s_bb = {}
    for n, w in (("abq", 512), ("abqc", 512), ("abk", 512), ("abkc", 512),
                 ("abv", 1024), ("abvc", 1024), ("bbgate", 64)):
        s_bb[n] = E(nc.sbuf_tensor("s_" + n, [1, w], BF16))
    s_tmp = E(nc.sbuf_tensor("s_tmp", [128, 512], F32))
    v_mu = E(nc.sbuf_tensor("v_mu", [1, 512], F32))
    v_m2 = E(nc.sbuf_tensor("v_m2", [1, 512], F32))
    v_t1 = E(nc.sbuf_tensor("v_t1", [1, 512], F32))
    v_r = E(nc.sbuf_tensor("v_r", [1, 512], F32))
    v_nm = E(nc.sbuf_tensor("v_nm", [1, 512], F32))

    def dma_fn(dst_ap, src_ap):
        def fn(e, dst=dst_ap, src=src_ap):
            return e.dma_start(dst, src)
        return fn

    for f in range(NF):
        B.ev("sync", dma_fn(XS[:, NT * f:NT * (f + 1)], d["x"][128 * f:128 * (f + 1), :]),
             incs=(("dS", 16),))
    x_loaded = C["dS"]

    def gload(dst_ap, src_ap, waits=()):
        B.ev("gpsimd", dma_fn(dst_ap, src_ap), waits=waits, incs=(("dG", 16),))
        return C["dG"]

    gload(s_gam[:, :], d["bgam"])
    gload(s_bet[:, :], d["bbet"])
    gload(s_eigr[:, :], d["beig"])
    for n in s_bb:
        gload(s_bb[n][:, :], d[n])
    small_loaded = C["dG"]

    B.ev("vector", lambda e: e.memset(ones_bf[:, :], 1.0))
    B.ev("vector", lambda e: e.memset(onesr_bf[:, :], 1.0))
    B.ev("vector", lambda e: e.memset(onesb_f[:, :], 1.0), incs=(("vV", 1),))
    B.ev("scalar", lambda e: e.activation(s_eig[:, :], s_eigr[:, :], ACT.Tanh),
         waits=(("dG", small_loaded),), incs=(("sS", 1),))

    def ln_chain(src_sl, dst_sl, psums, scratch, state, prewaits=(), gamma=False):
        p_s1, p_s2, p_rbc, p_nm = psums
        sxb = scratch
        w = list(prewaits)
        if state.get("stats2"):
            w.append(("tM", state["stats2"]))
        for f in range(NF):
            B.ev("scalar",
                 (lambda e, f=f: e.activation(
                     sxb[:, 512 * f:512 * (f + 1)], src_sl(f), ACT.Copy)),
                 waits=tuple(w) if f == 0 else (),
                 incs=((("sS", 1),) if f == NF - 1 else ()))
        xb_done = C["sS"]

        w = [("sS", xb_done)]
        if state.get("muvar"):
            w.append(("vV", state["muvar"]))

        def stats1(e):
            for f in range(NF):
                ins = e.matmul(p_s1[:, :], ones_bf[:, :], sxb[:, 512 * f:512 * (f + 1)],
                               start=(f == 0), stop=(f == NF - 1))
            return ins
        B.ev("tensor", stats1, waits=tuple(w), incs=(("tM", 1),))
        state["stats1"] = C["tM"]

        for f in range(NF):
            B.ev("scalar",
                 (lambda e, f=f: e.activation(
                     sxb[:, 512 * f:512 * (f + 1)], src_sl(f), ACT.Square)),
                 waits=(("tM", state["stats1"]),) if f == 0 else (),
                 incs=((("sS", 1),) if f == NF - 1 else ()))
        xq_done = C["sS"]

        def stats2(e):
            for f in range(NF):
                ins = e.matmul(p_s2[:, :], ones_bf[:, :], sxb[:, 512 * f:512 * (f + 1)],
                               start=(f == 0), stop=(f == NF - 1))
            return ins
        B.ev("tensor", stats2, waits=(("sS", xq_done),), incs=(("tM", 1),))
        state["stats2"] = C["tM"]

        def muvar(e):
            e.tensor_single_scalar(v_mu[:, :], p_s1[:, :], 1.0 / 1024.0, OP.mult)
            e.tensor_single_scalar(v_m2[:, :], p_s2[:, :], 1.0 / 1024.0, OP.mult)
            e.tensor_mul(v_t1[:, :], v_mu[:, :], v_mu[:, :])
            e.tensor_sub(v_m2[:, :], v_m2[:, :], v_t1[:, :])
            return e.tensor_single_scalar(v_t1[:, :], v_m2[:, :], 1e-5, OP.add)
        B.ev("vector", muvar, waits=(("tM", state["stats2"]),), incs=(("vV", 1),))
        state["muvar"] = C["vV"]

        def rstd(e):
            e.activation(v_m2[:, :], v_t1[:, :], ACT.Ln)
            return e.activation(v_r[:, :], v_m2[:, :], ACT.Exp, scale=-0.5)
        B.ev("scalar", rstd, waits=(("vV", state["muvar"]),), incs=(("sS", 1),))
        rstd_done = C["sS"]

        B.ev("vector", lambda e: e.tensor_mul(v_nm[:, :], v_mu[:, :], v_r[:, :]),
             waits=(("sS", rstd_done),), incs=(("vV", 1),))
        nm_done = C["vV"]

        def bcast(e):
            e.matmul(p_rbc[:, :], onesb_f[:, :], v_r[:, :], start=True, stop=True)
            return e.matmul(p_nm[:, :], onesb_f[:, :], v_nm[:, :], start=True, stop=True)
        B.ev("tensor", bcast, waits=(("vV", nm_done),), incs=(("tM", 1),))
        bcast_done = C["tM"]

        for f in range(NF):
            if not gamma:
                def apply(e, f=f):
                    e.tensor_mul(s_tmp[:, :], src_sl(f), p_rbc[:, :])
                    return e.tensor_sub(dst_sl(f), s_tmp[:, :], p_nm[:, :])
            else:
                def apply(e, f=f):
                    e.tensor_mul(s_tmp[:, :], src_sl(f), p_rbc[:, :])
                    e.tensor_sub(s_tmp[:, :], s_tmp[:, :], p_nm[:, :])
                    return e.tensor_scalar(dst_sl(f), s_tmp[:, :],
                                           s_gam[:, f:f + 1], s_bet[:, f:f + 1],
                                           OP.mult, OP.add)
            B.ev("vector", apply,
                 waits=(("tM", bcast_done),) if f == 0 else (),
                 incs=((("vV", 1),) if f == NF - 1 else ()))
        return C["vV"]

    def mk_ln_psums(ps):
        return (ps.enter_context(nc.psum_tensor(f"ps{uid()}", [1, 512], F32)),
                ps.enter_context(nc.psum_tensor(f"ps{uid()}", [1, 512], F32)),
                ps.enter_context(nc.psum_tensor(f"ps{uid()}", [128, 512], F32)),
                ps.enter_context(nc.psum_tensor(f"ps{uid()}", [128, 512], F32)))

    with contextlib.ExitStack() as attn_scope:
        A = attn_scope.enter_context
        VT = A(nc.sbuf_tensor("VT", [128, NKC * 1024], BF16))
        QF = A(nc.sbuf_tensor("QF", [128, 4 * SP], BF16))
        KF = A(nc.sbuf_tensor("KF", [128, 4 * SP], BF16))
        xln_scope = contextlib.ExitStack()
        XLN = xln_scope.enter_context(nc.sbuf_tensor("XLN", [128, NF * SP], BF16))

        for f in range(NF):
            B.ev("vector",
                 (lambda e, f=f: e.memset(XLN[:, SP * f + NT:SP * (f + 1)], 0.0)),
                 incs=((("vV", 1),) if f == NF - 1 else ()))

        with contextlib.ExitStack() as ps:
            psums = mk_ln_psums(ps)
            sxb = ps.enter_context(nc.sbuf_tensor(f"sxb{uid()}", [128, 4096], BF16))
            B.barrier()
            st = {}
            for tb in range(4):
                ln_chain(lambda f, tb=tb: XS[:, NT * f + 512 * tb: NT * f + 512 * (tb + 1)],
                         lambda f, tb=tb: XLN[:, SP * f + 512 * tb: SP * f + 512 * (tb + 1)],
                         psums, sxb, st,
                         prewaits=(("dS", x_loaded),) if tb == 0 else ())
        B.barrier()

        for nb in range(2):
            with contextlib.ExitStack() as p2a:
                aw_v = p2a.enter_context(nc.sbuf_tensor(f"awv{uid()}", [128, 8 * 512], BF16))
                aw_vc = p2a.enter_context(nc.sbuf_tensor(f"awvc{uid()}", [128, 8 * 512], BF16))
                s_sig = p2a.enter_context(nc.sbuf_tensor(f"ssig{uid()}", [128, 512], F32))
                p_v = [p2a.enter_context(nc.psum_tensor(f"ps{uid()}", [128, 512], F32)) for _ in range(2)]
                p_vc = [p2a.enter_context(nc.psum_tensor(f"ps{uid()}", [128, 512], F32)) for _ in range(2)]

                gload(aw_v[:, :].rearrange("p (c m) -> p c m", c=8),
                      d["awv"][:, 512 * nb:512 * (nb + 1)].rearrange("(c p) m -> p c m", p=128))
                awv_loaded = gload(aw_vc[:, :].rearrange("p (c m) -> p c m", c=8),
                                   d["awvc"][:, 512 * nb:512 * (nb + 1)].rearrange("(c p) m -> p c m", p=128))

                evac_done = {}
                for tt in range(NKC):
                    rows = 16 if tt == 16 else 128
                    par = tt % 2
                    w = [("dG", awv_loaded)]
                    if par in evac_done:
                        w.append(("vV", evac_done[par]))

                    def vmms(e, tt=tt, rows=rows, par=par, nb=nb, aw_v=aw_v, aw_vc=aw_vc, p_v=p_v, p_vc=p_vc):
                        tok0 = 128 * tt
                        ins = None
                        for ps_, aw_, ab_ in ((p_v[par], aw_v, s_bb["abv"]),
                                              (p_vc[par], aw_vc, s_bb["abvc"])):
                            for f in range(NF):
                                e.matmul(ps_[0:rows, :],
                                         XLN[:, SP * f + tok0: SP * f + tok0 + rows],
                                         aw_[:, 512 * f:512 * (f + 1)],
                                         start=(f == 0), stop=False)
                            ins = e.matmul(ps_[0:rows, :],
                                           onesr_bf[0:1, 0:rows],
                                           ab_[0:1, 512 * nb:512 * (nb + 1)],
                                           start=False, stop=True)
                        return ins
                    B.ev("tensor", vmms, waits=tuple(w), incs=(("tM", 1),))
                    mm_done = C["tM"]

                    _w_sig = [("tM", mm_done)]
                    if evac_done:
                        _w_sig.append(("vV", max(evac_done.values())))
                    B.ev("scalar",
                         (lambda e, rows=rows, par=par, s_sig=s_sig, p_vc=p_vc: e.activation(
                             s_sig[0:rows, :], p_vc[par][0:rows, :], ACT.Sigmoid)),
                         waits=tuple(_w_sig), incs=(("sS", 1),))
                    sg_done = C["sS"]

                    B.ev("vector",
                         (lambda e, tt=tt, rows=rows, par=par, nb=nb, p_v=p_v, s_sig=s_sig: e.tensor_mul(
                             VT[0:rows, 1024 * tt + 512 * nb:1024 * tt + 512 * (nb + 1)],
                             p_v[par][0:rows, :], s_sig[0:rows, :])),
                         waits=(("sS", sg_done),), incs=(("vV", 1),))
                    evac_done[par] = C["vV"]
            B.barrier()

        for dst, wname, wcname, bname, bcname in (
                (QF, "awq", "awqc", "abq", "abqc"),
                (KF, "awk", "awkc", "abk", "abkc")):
            with contextlib.ExitStack() as p2b:
                awW = p2b.enter_context(nc.sbuf_tensor(f"aw{uid()}", [128, 8 * 512], BF16))
                awC = p2b.enter_context(nc.sbuf_tensor(f"aw{uid()}", [128, 8 * 512], BF16))
                gload(awW[:, :].rearrange("p (c m) -> p c m", c=8),
                      d[wname].rearrange("(c p) m -> p c m", p=128))
                qk_loaded = gload(awC[:, :].rearrange("p (c m) -> p c m", c=8),
                                  d[wcname].rearrange("(c p) m -> p c m", p=128))
                s_sig2 = p2b.enter_context(nc.sbuf_tensor(f"sg2{uid()}", [128, 512], F32))
                p_q = [p2b.enter_context(nc.psum_tensor(f"ps{uid()}", [128, 512], F32)) for _ in range(2)]
                p_qc = [p2b.enter_context(nc.psum_tensor(f"ps{uid()}", [128, 512], F32)) for _ in range(2)]

                evac_done = {}
                it = 0
                for oc in range(4):
                    for tb in range(5):
                        ncols = 16 if tb == 4 else 512
                        tok0 = 512 * tb
                        par = it % 2
                        w = [("dG", qk_loaded)]
                        if par in evac_done:
                            w.append(("vV", evac_done[par]))

                        def qmms(e, oc=oc, tok0=tok0, ncols=ncols, par=par, awW=awW, awC=awC, p_q=p_q, p_qc=p_qc, bname=bname, bcname=bcname):
                            ins = None
                            for ps_, aw_, ab_n in ((p_q[par], awW, bname),
                                                   (p_qc[par], awC, bcname)):
                                for f in range(NF):
                                    e.matmul(ps_[:, 0:ncols],
                                             aw_[:, 512 * f + 128 * oc: 512 * f + 128 * (oc + 1)],
                                             XLN[:, SP * f + tok0: SP * f + tok0 + ncols],
                                             start=(f == 0), stop=False)
                                ins = e.matmul(ps_[:, 0:ncols],
                                               s_bb[ab_n][0:1, 128 * oc:128 * (oc + 1)],
                                               onesr_bf[0:1, 0:ncols],
                                               start=False, stop=True)
                            return ins
                        B.ev("tensor", qmms, waits=tuple(w), incs=(("tM", 1),))
                        mm_done = C["tM"]

                        _w_sig = [("tM", mm_done)]
                        if evac_done:
                            _w_sig.append(("vV", max(evac_done.values())))
                        B.ev("scalar",
                             (lambda e, ncols=ncols, par=par, s_sig2=s_sig2, p_qc=p_qc: e.activation(
                                 s_sig2[:, 0:ncols], p_qc[par][:, 0:ncols], ACT.Sigmoid)),
                             waits=tuple(_w_sig), incs=(("sS", 1),))
                        sg_done = C["sS"]

                        B.ev("vector",
                             (lambda e, dst=dst, oc=oc, tok0=tok0, ncols=ncols, par=par,
                                     p_q=p_q, s_sig2=s_sig2:
                              e.tensor_mul(dst[:, SP * oc + tok0: SP * oc + tok0 + ncols],
                                           p_q[par][:, 0:ncols], s_sig2[:, 0:ncols])),
                             waits=(("sS", sg_done),), incs=(("vV", 1),))
                        evac_done[par] = C["vV"]
                        it += 1
            B.barrier()

        xln_scope.close()

        if os.environ.get("KSTOP", "") == "qonly":
            _p3_skip = True
        else:
            _p3_skip = False
        with contextlib.ExitStack() as p3:
          if not _p3_skip:
              PT = [p3.enter_context(nc.sbuf_tensor(f"PT{i}", [128, NKC * 512], BF16)) for i in range(2)]
              v_rz = p3.enter_context(nc.sbuf_tensor("v_rz", [1, 512], F32))
              s_ct = p3.enter_context(nc.sbuf_tensor("s_ct", [128, 512], F32))
              s_zb = p3.enter_context(nc.sbuf_tensor("s_zb", [128, 512], F32))
              p_st = [p3.enter_context(nc.psum_tensor(f"ps{uid()}", [128, 1024], F32)) for _ in range(2)]
              p_z = p3.enter_context(nc.psum_tensor(f"ps{uid()}", [1, 512], F32))
              p_ctx = [p3.enter_context(nc.psum_tensor(f"ps{uid()}", [128, 512], F32)) for _ in range(2)]
              p_zb = p3.enter_context(nc.psum_tensor(f"ps{uid()}", [128, 512], F32))

              exp_done_all = []
              recip_done = {}
              evac_done = {}
              zstop_done = {}
              hdone = {}

              gidx = 0
              for qblk in range(4):
                  for h in range(NH):
                      g = gidx
                      gidx += 1
                      pr = g // 2
                      hc = h // 4
                      hr = 32 * (h % 4)
                      q0 = 512 * qblk
                      rowoff = 64 * (h % 2)

                      for kp in range(9):
                          P = 9 * g + kp
                          w = []
                          if P >= 2:
                              w.append(("sS", exp_done_all[P - 2]))

                          def stmms(e, hc=hc, hr=hr, q0=q0, kp=kp, P=P):
                              ins = None
                              for ki in range(2):
                                  kc = 2 * kp + ki
                                  if kc > 16:
                                      break
                                  rows = 16 if kc == 16 else 128
                                  ins = e.matmul(
                                      p_st[P % 2][0:rows, 512 * ki:512 * ki + 512],
                                      KF[hr:hr + 32, SP * hc + 128 * kc: SP * hc + 128 * kc + rows],
                                      QF[hr:hr + 32, SP * hc + q0: SP * hc + q0 + 512],
                                      start=True, stop=True, tile_position=(hr, 0))
                              return ins
                          B.ev("tensor", stmms, waits=tuple(w), incs=(("tM", 1),))
                          st_done = C["tM"]

                          w = [("tM", st_done)]
                          if g >= 2 and kp == 0:
                              w.append(("tM", hdone[g - 2]))

                          def expev(e, g=g, kp=kp, P=P):
                              kc0 = 2 * kp
                              if kc0 == 16:
                                  return e.activation(PT[g % 2][0:16, 512 * 16:512 * 17],
                                                      p_st[P % 2][0:16, 0:512],
                                                      ACT.Exp, scale=0.125)
                              return e.activation(PT[g % 2][:, 512 * kc0:512 * (kc0 + 2)],
                                                  p_st[P % 2][:, :], ACT.Exp, scale=0.125)
                          B.ev("scalar", expev, waits=tuple(w), incs=(("sS", 1),))
                          exp_done_all.append(C["sS"])

                      w = [("sS", exp_done_all[9 * g + 8])]
                      if g >= 1:
                          w.append(("vV", recip_done[g - 1]))
                      if pr >= 2 and h % 2 == 0:
                          w.append(("vV", evac_done[pr - 2]))

                      def zctx(e, g=g, h=h, rowoff=rowoff):
                          ins = None
                          for kc in range(NKC):
                              rows = 16 if kc == 16 else 128
                              pt = PT[g % 2][0:rows, 512 * kc:512 * kc + 512]
                              e.matmul(p_z[:, :], ones_bf[0:rows, :], pt,
                                       start=(kc == 0), stop=(kc == NKC - 1))
                              ins = e.matmul(
                                  p_ctx[(g // 2) % 2][rowoff:rowoff + 64, :],
                                  VT[0:rows, 1024 * kc + 64 * h: 1024 * kc + 64 * h + 64],
                                  pt, start=(kc == 0), stop=(kc == NKC - 1))
                          return ins
                      B.ev("tensor", zctx, waits=tuple(w), incs=(("tM", 1),))
                      zstop_done[g] = C["tM"]

                      B.ev("vector",
                           (lambda e: e.reciprocal(v_rz[:, :], p_z[:, :])),
                           waits=(("tM", zstop_done[g]),), incs=(("vV", 1),))
                      recip_done[g] = C["vV"]

                      w = [("vV", recip_done[g])]
                      if pr >= 1 and h % 2 == 0:
                          w.append(("vV", evac_done[pr - 1]))

                      def zbev(e, rowoff=rowoff):
                          return e.matmul(p_zb[rowoff:rowoff + 64, :],
                                          onesb_f[0:1, 0:64], v_rz[:, :],
                                          start=True, stop=True)
                      B.ev("tensor", zbev, waits=tuple(w), incs=(("tM", 1),))
                      hdone[g] = C["tM"]

                      if h % 2 == 1:
                          fch = (h - 1) // 2

                          def evac(e, fch=fch, qblk=qblk, g=g):
                              e.tensor_copy(s_zb[:, :], p_zb[:, :])
                              e.tensor_mul(s_ct[:, :], p_ctx[(g // 2) % 2][:, :], s_zb[:, :])
                              sl = XS[:, NT * fch + 512 * qblk: NT * fch + 512 * (qblk + 1)]
                              return e.tensor_add(sl, sl, s_ct[:, :])
                          B.ev("vector", evac, waits=(("tM", hdone[g]),), incs=(("vV", 1),))
                          evac_done[pr] = C["vV"]
        B.barrier()

    _KSTOP = os.environ.get("KSTOP", "")

    def emit_ffn():
        with contextlib.ExitStack() as fs:
            A = fs.enter_context
            XLNf = A(nc.sbuf_tensor(f"XLNf{uid()}", [128, NF * 1024], BF16))
            HB = A(nc.sbuf_tensor(f"HB{uid()}", [128, 32 * 1024], BF16))
            wg_buf = [A(nc.sbuf_tensor(f"wg{uid()}", [128, 8 * 256], BF16)) for _ in range(2)]
            wu_buf = [A(nc.sbuf_tensor(f"wu{uid()}", [128, 8 * 256], BF16)) for _ in range(2)]
            wd_buf = [A(nc.sbuf_tensor(f"wd{uid()}", [128, 16 * 128], BF16)) for _ in range(2)]
            s_sg = A(nc.sbuf_tensor(f"sg{uid()}", [128, 1024], F32))

            free_tm = {}
            free_vv = {}
            free_ss = {}
            dma_done = {}

            for half in range(2):
                with contextlib.ExitStack() as ps:
                    psums = mk_ln_psums(ps)
                    sxb = ps.enter_context(nc.sbuf_tensor(f"sxb{uid()}", [128, 4096], BF16))
                    B.barrier()
                    st = {}
                    for tb in range(2):
                        ln_chain(
                            lambda f, tb=tb, half=half: XS[:, NT * f + 1024 * half + 512 * tb:
                                                           NT * f + 1024 * half + 512 * (tb + 1)],
                            lambda f, tb=tb: XLNf[:, 1024 * f + 512 * tb: 1024 * f + 512 * (tb + 1)],
                            psums, sxb, st)
                B.barrier()

                with contextlib.ExitStack() as ps:
                    p_g = ps.enter_context(nc.psum_tensor(f"ps{uid()}", [128, 1024], F32))
                    p_u = ps.enter_context(nc.psum_tensor(f"ps{uid()}", [128, 1024], F32))
                    p_y = [ps.enter_context(nc.psum_tensor(f"ps{uid()}", [128, 1024], F32))
                           for _ in range(2)]

                    for e_ in range(8):
                        def dma_wgwu(j16, e_=e_):
                            for nm_, dd, buf in (("wg", d["fwg"], wg_buf), ("wu", d["fwu"], wu_buf)):
                                w = []
                                if (nm_, j16 % 2) in free_tm:
                                    w.append(("tM", free_tm[(nm_, j16 % 2)]))

                                def fn(e, dd=dd, e_=e_, j16=j16, buf=buf):
                                    return e.dma_start(
                                        buf[j16 % 2][:, :].rearrange("p (c m) -> p c m", c=8),
                                        dd[e_][:, 256 * j16:256 * (j16 + 1)].rearrange(
                                            "(c p) m -> p c m", p=128))
                                B.ev("gpsimd", fn, waits=tuple(w), incs=(("dG", 16),))
                                dma_done[(nm_, j16 % 2)] = C["dG"]

                        def dma_wd(ocmh, e_=e_):
                            oc, mh = ocmh // 2, ocmh % 2
                            w = []
                            if ("wd", ocmh % 2) in free_tm:
                                w.append(("tM", free_tm[("wd", ocmh % 2)]))

                            def fn(e, e_=e_, oc=oc, mh=mh, ocmh=ocmh):
                                return e.dma_start(
                                    wd_buf[ocmh % 2][:, :].rearrange("p (c m) -> p c m", c=16),
                                    d["fwd"][e_][oc][16 * mh:16 * (mh + 1)].transpose([1, 0, 2]))
                            B.ev("gpsimd", fn, waits=tuple(w), incs=(("dG", 16),))
                            dma_done[("wd", ocmh % 2)] = C["dG"]

                        dma_wgwu(0)
                        dma_wgwu(1)
                        dma_wd(0)
                        dma_wd(1)

                        for m in range(32):
                            j16 = m // 2
                            mo = m % 2
                            w = [("dG", dma_done[("wu", j16 % 2)])]
                            if ("pg",) in free_ss:
                                w.append(("sS", free_ss[("pg",)]))

                            def gmms(e, j16=j16, mo=mo, p_g=p_g):
                                ins = None
                                for f in range(NF):
                                    for tb in range(2):
                                        ins = e.matmul(
                                            p_g[:, 512 * tb:512 * (tb + 1)],
                                            wg_buf[j16 % 2][:, 256 * f + 128 * mo: 256 * f + 128 * (mo + 1)],
                                            XLNf[:, 1024 * f + 512 * tb: 1024 * f + 512 * (tb + 1)],
                                            start=(f == 0), stop=(f == NF - 1))
                                return ins
                            B.ev("tensor", gmms, waits=tuple(w), incs=(("tM", 1),))
                            g_done = C["tM"]

                            w = []
                            if ("pu",) in free_vv:
                                w.append(("vV", free_vv[("pu",)]))

                            def umms(e, j16=j16, mo=mo, p_u=p_u):
                                ins = None
                                for f in range(NF):
                                    for tb in range(2):
                                        ins = e.matmul(
                                            p_u[:, 512 * tb:512 * (tb + 1)],
                                            wu_buf[j16 % 2][:, 256 * f + 128 * mo: 256 * f + 128 * (mo + 1)],
                                            XLNf[:, 1024 * f + 512 * tb: 1024 * f + 512 * (tb + 1)],
                                            start=(f == 0), stop=(f == NF - 1))
                                return ins
                            B.ev("tensor", umms, waits=tuple(w), incs=(("tM", 1),))
                            u_done = C["tM"]
                            if mo == 1:
                                free_tm[("wg", j16 % 2)] = g_done
                                free_tm[("wu", j16 % 2)] = u_done
                                if j16 + 2 < 16:
                                    dma_wgwu(j16 + 2)

                            w = [("tM", g_done)]
                            if ("sg",) in free_vv:
                                w.append(("vV", free_vv[("sg",)]))
                            B.ev("scalar",
                                 (lambda e, p_g=p_g: e.activation(s_sg[:, :], p_g[:, :], ACT.Silu)),
                                 waits=tuple(w), incs=(("sS", 1),))
                            free_ss[("pg",)] = C["sS"]

                            w = [("sS", free_ss[("pg",)]), ("tM", u_done)]
                            if ("HB",) in free_tm:
                                w.append(("tM", free_tm[("HB",)]))

                            B.ev("vector",
                                 (lambda e, m=m, p_u=p_u: e.tensor_mul(
                                     HB[:, 1024 * m:1024 * (m + 1)], s_sg[:, :], p_u[:, :])),
                                 waits=tuple(w), incs=(("vV", 1),))
                            free_vv[("pu",)] = C["vV"]
                            free_vv[("sg",)] = C["vV"]
                        hb_ready = C["vV"]
                        free_tm.pop(("HB",), None)

                        for oc in range(8):
                            for mh in range(2):
                                ocmh = oc * 2 + mh
                                w = [("dG", dma_done[("wd", ocmh % 2)])]
                                if mh == 0:
                                    w.append(("vV", hb_ready))
                                    if ("py", oc % 2) in free_vv:
                                        w.append(("vV", free_vv[("py", oc % 2)]))

                                def ymms(e, oc=oc, mh=mh, ocmh=ocmh, p_y=p_y):
                                    ins = None
                                    for mm_ in range(16):
                                        m = 16 * mh + mm_
                                        for tb in range(2):
                                            ins = e.matmul(
                                                p_y[oc % 2][:, 512 * tb:512 * (tb + 1)],
                                                wd_buf[ocmh % 2][:, 128 * mm_:128 * (mm_ + 1)],
                                                HB[:, 1024 * m + 512 * tb: 1024 * m + 512 * (tb + 1)],
                                                start=(m == 0), stop=(m == 31))
                                    return ins
                                B.ev("tensor", ymms, waits=tuple(w), incs=(("tM", 1),))
                                free_tm[("wd", ocmh % 2)] = C["tM"]
                                y_done = C["tM"]
                                if ocmh + 2 < 16:
                                    dma_wd(ocmh + 2)
                            if oc == 7:
                                free_tm[("HB",)] = C["tM"]

                            def yev(e, oc=oc, half=half, p_y=p_y):
                                sl = XS[:, NT * oc + 1024 * half: NT * oc + 1024 * (half + 1)]
                                return e.tensor_add(sl, sl, p_y[oc % 2][:, :])
                            B.ev("vector", yev, waits=(("tM", y_done),), incs=(("vV", 1),))
                            free_vv[("py", oc % 2)] = C["vV"]
                B.barrier()

    if _KSTOP not in ("attn", "qonly"):
        emit_ffn()

    if _KSTOP not in ("attn", "ffn1", "qonly"):
      with contextlib.ExitStack() as bs:
          A = bs.enter_context
          xpn = A(nc.sbuf_tensor("xpn", [128, NF * NT], BF16))
          B_t = A(nc.sbuf_tensor("B_t", [48, NT], F32))
          CU = A(nc.sbuf_tensor("CU", [16, NT], F32))
          RC = A(nc.sbuf_tensor("RC", [16, NT], F32))
          s_inda = A(nc.sbuf_tensor("s_inda", [16, 1024], F32))
          s_indb = A(nc.sbuf_tensor("s_indb", [48, 1024], F32))
          gload(s_inda[:, :], d["inda"])
          ind_loaded = gload(s_indb[:, :], d["indb"])

          a_scope = contextlib.ExitStack()
          A_t = a_scope.enter_context(nc.sbuf_tensor("A_t", [16, NT], F32))

          with contextlib.ExitStack() as b2s:
              A2 = b2s.enter_context
              XLNb = A2(nc.sbuf_tensor("XLNb", [128, 8 * 512], BF16))
              bw_in = A2(nc.sbuf_tensor("bw_in", [128, 8 * 1024], BF16))
              bw_gate = A2(nc.sbuf_tensor("bw_gate", [128, 8 * 64], BF16))
              XP = A2(nc.sbuf_tensor("XP", [128, 8 * 512], F32))
              sxb = A2(nc.sbuf_tensor(f"sxb{uid()}", [128, 4096], BF16))

              gload(bw_in[:, :].rearrange("p (c m) -> p c m", c=8),
                    d["bwin"].rearrange("(c p) m -> p c m", p=128))
              bw_loaded = gload(bw_gate[:, :].rearrange("p (c m) -> p c m", c=8),
                                d["bwgate"].rearrange("(c p) m -> p c m", p=128))

              with contextlib.ExitStack() as ps:
                  p_xp = [ps.enter_context(nc.psum_tensor(f"ps{uid()}", [128, 512], F32)) for _ in range(2)]
                  p_gt = ps.enter_context(nc.psum_tensor(f"ps{uid()}", [64, 512], F32))
                  psums = mk_ln_psums(ps)
                  B.barrier()

                  lnst = {}
                  free_xp = {}
                  xpmm_done = {}
                  sigg_done = 0
                  for tb in range(4):
                      t0 = 512 * tb
                      pw = []
                      if tb == 0:
                          pw.append(("dG", bw_loaded))
                      if tb >= 1:
                          pw.append(("tM", xpmm_done[tb - 1]))
                      apply3_done = ln_chain(
                          lambda f, t0=t0: XS[:, NT * f + t0: NT * f + t0 + 512],
                          lambda f: XLNb[:, 512 * f:512 * (f + 1)],
                          psums, sxb, lnst, prewaits=tuple(pw))

                      for oc in range(8):
                          slot = oc % 2
                          w = [("vV", apply3_done)]
                          if slot in free_xp:
                              w.append(("vV", free_xp[slot][0]))
                              w.append(("sS", free_xp[slot][1]))

                          def xpmm(e, oc=oc, slot=slot):
                              ins = None
                              for f in range(NF):
                                  ins = e.matmul(p_xp[slot][:, :],
                                                 bw_in[:, 1024 * f + 128 * oc: 1024 * f + 128 * (oc + 1)],
                                                 XLNb[:, 512 * f:512 * (f + 1)],
                                                 start=(f == 0), stop=(f == NF - 1))
                              return ins
                          B.ev("tensor", xpmm, waits=tuple(w), incs=(("tM", 1),))
                          mmv = C["tM"]

                          def xev(e, oc=oc, slot=slot):
                              return e.tensor_copy(XP[:, 512 * oc:512 * (oc + 1)], p_xp[slot][:, :])
                          B.ev("vector", xev, waits=(("tM", mmv),), incs=(("vV", 1),))
                          vvv = C["vV"]

                          def xbev(e, oc=oc, slot=slot):
                              return e.activation(sxb[:, 512 * oc:512 * (oc + 1)], p_xp[slot][:, :],
                                                  ACT.Copy)
                          B.ev("scalar", xbev, waits=(("tM", mmv), ("vV", vvv)),
                             incs=(("sS", 1),))
                          free_xp[slot] = (vvv, C["sS"])
                      xpmm_done[tb] = C["tM"]
                      sS_evacs = C["sS"]
                      vV_evacs = C["vV"]

                      def stats1(e):
                          for f in range(NF):
                              ins = e.matmul(psums[0][:, :], ones_bf[:, :], sxb[:, 512 * f:512 * (f + 1)],
                                             start=(f == 0), stop=(f == NF - 1))
                          return ins
                      B.ev("tensor", stats1, waits=(("sS", sS_evacs), ("vV", lnst.get("muvar", 0))),
                           incs=(("tM", 1),))
                      lnst["stats1"] = C["tM"]

                      for f in range(NF):
                          B.ev("scalar",
                               (lambda e, f=f: e.activation(sxb[:, 512 * f:512 * (f + 1)],
                                                            XP[:, 512 * f:512 * (f + 1)], ACT.Square)),
                               waits=(("tM", lnst["stats1"]), ("vV", vV_evacs)) if f == 0 else (),
                               incs=((("sS", 1),) if f == NF - 1 else ()))
                      xq_done = C["sS"]

                      def stats2(e):
                          for f in range(NF):
                              ins = e.matmul(psums[1][:, :], ones_bf[:, :], sxb[:, 512 * f:512 * (f + 1)],
                                             start=(f == 0), stop=(f == NF - 1))
                          return ins
                      B.ev("tensor", stats2, waits=(("sS", xq_done),), incs=(("tM", 1),))
                      lnst["stats2"] = C["tM"]

                      def muvar(e):
                          e.tensor_single_scalar(v_mu[:, :], psums[0][:, :], 1.0 / 1024.0, OP.mult)
                          e.tensor_single_scalar(v_m2[:, :], psums[1][:, :], 1.0 / 1024.0, OP.mult)
                          e.tensor_mul(v_t1[:, :], v_mu[:, :], v_mu[:, :])
                          e.tensor_sub(v_m2[:, :], v_m2[:, :], v_t1[:, :])
                          return e.tensor_single_scalar(v_t1[:, :], v_m2[:, :], 1e-5, OP.add)
                      B.ev("vector", muvar, waits=(("tM", lnst["stats2"]),), incs=(("vV", 1),))
                      lnst["muvar"] = C["vV"]

                      def rstd(e):
                          e.activation(v_m2[:, :], v_t1[:, :], ACT.Ln)
                          return e.activation(v_r[:, :], v_m2[:, :], ACT.Exp, scale=-0.5)
                      B.ev("scalar", rstd, waits=(("vV", lnst["muvar"]),), incs=(("sS", 1),))
                      rstd_done = C["sS"]

                      B.ev("vector", lambda e: e.tensor_mul(v_nm[:, :], v_mu[:, :], v_r[:, :]),
                           waits=(("sS", rstd_done),), incs=(("vV", 1),))
                      nm_done = C["vV"]

                      def bcast(e):
                          e.matmul(psums[2][:, :], onesb_f[:, :], v_r[:, :], start=True, stop=True)
                          return e.matmul(psums[3][:, :], onesb_f[:, :], v_nm[:, :], start=True, stop=True)
                      B.ev("tensor", bcast, waits=(("vV", nm_done),), incs=(("tM", 1),))
                      bcast_done = C["tM"]

                      for oc in range(NF):
                          def apply(e, oc=oc, t0=t0):
                              e.tensor_mul(s_tmp[:, :], XP[:, 512 * oc:512 * (oc + 1)], psums[2][:, :])
                              e.tensor_sub(s_tmp[:, :], s_tmp[:, :], psums[3][:, :])
                              return e.tensor_scalar(xpn[:, NT * oc + t0: NT * oc + t0 + 512],
                                                     s_tmp[:, :], s_gam[:, oc:oc + 1],
                                                     s_bet[:, oc:oc + 1], OP.mult, OP.add)
                          B.ev("vector", apply,
                               waits=(("tM", bcast_done),) if oc == 0 else (),
                               incs=((("vV", 1),) if oc == NF - 1 else ()))
                      apply_done = C["vV"]

                      def gmm(e, t0=t0):
                          for oc in range(NF):
                              e.matmul(p_gt[:, :],
                                       bw_gate[:, 64 * oc:64 * (oc + 1)],
                                       xpn[:, NT * oc + t0: NT * oc + t0 + 512],
                                       start=(oc == 0), stop=False)
                          return e.matmul(p_gt[:, :], s_bb["bbgate"][:, :], onesr_bf[:, 0:512],
                                          start=False, stop=True)
                      w = [("vV", apply_done)]
                      if sigg_done:
                          w.append(("sS", sigg_done))
                      B.ev("tensor", gmm, waits=tuple(w), incs=(("tM", 1),))
                      gmm_done = C["tM"]

                      def sigg(e, t0=t0):
                          e.activation(A_t[0:16, t0:t0 + 512], p_gt[0:16, :], ACT.Sigmoid)
                          return e.activation(B_t[32:48, t0:t0 + 512], p_gt[32:48, :], ACT.Sigmoid)
                      B.ev("scalar", sigg, waits=(("tM", gmm_done), ("dG", ind_loaded)),
                           incs=(("sS", 1),))
                      sigg_done = C["sS"]
              B.barrier()

          with contextlib.ExitStack() as b3s:
              CP_ = b3s.enter_context(nc.sbuf_tensor("CP_", [16, NT], F32))
              G1 = b3s.enter_context(nc.sbuf_tensor("G1", [16, NT], F32))

              def b3(e):
                  e.tensor_single_scalar(CP_[:, :], A_t[0:16, :], s_eig[:, :], OP.mult)
                  e.tensor_copy(CU[:, :], CP_[:, :])
                  for p in range(1, 16):
                      e.tensor_mul(CU[:, p::16], CU[:, p - 1::16], CP_[:, p::16])
                  e.tensor_copy(A_t[0:16, 1:NT], CU[:, 0:NT - 1])
                  e.tensor_scalar(A_t[0:16, 0::16], CU[:, 0::16], 0.0, 1.0, OP.mult, OP.add)
                  e.tensor_single_scalar(G1[:, :], A_t[0:16, :], 1e-8, OP.is_gt)
                  e.tensor_single_scalar(RC[:, :], A_t[0:16, :], -1e-8, OP.is_lt)
                  e.tensor_max(G1[:, :], G1[:, :], RC[:, :])
                  e.tensor_mul(RC[:, :], A_t[0:16, :], G1[:, :])
                  e.tensor_scalar(CP_[:, :], G1[:, :], -1.0, 1.0, OP.mult, OP.add)
                  e.tensor_add(RC[:, :], RC[:, :], CP_[:, :])
                  e.reciprocal(CP_[:, :], RC[:, :])
                  return e.tensor_mul(RC[:, :], CP_[:, :], G1[:, :])
              B.ev("vector", b3, incs=(("vV", 1),))
              B.barrier()
          a_scope.close()

          HT = NT // 2
          HALL = A(nc.sbuf_tensor("HALL", [128, NF * NT], BF16))
          b4s = contextlib.ExitStack()
          A4 = b4s.enter_context
          mask16 = A4(nc.sbuf_tensor("mask16", [128, HT], F32))
          b_buf = A4(nc.sbuf_tensor("b_buf", [128, HT], F32))
          c_buf = A4(nc.sbuf_tensor("c_buf", [128, HT], F32))
          S_buf = A4(nc.sbuf_tensor("S_buf", [128, HT], F32))
          t_buf = A4(nc.sbuf_tensor("t_buf", [128, HT], F32))
          carry = A4(nc.sbuf_tensor("carry", [128, 132], F32))
          s_bcb = A4(nc.sbuf_tensor("s_bcb", [128, HT], BF16))

          def mkmask(e):
              e.memset(mask16[:, :], 1.0)
              e.memset(mask16[:, 0::16], 0.0)
              return e.memset(carry[:, 0:1], 0.0)
          B.ev("vector", mkmask, incs=(("vV", 1),))

          with contextlib.ExitStack() as ps:
              p_bc = ps.enter_context(nc.psum_tensor(f"ps{uid()}", [128, HT], F32))
              for f in range(NF):
                  for hf in range(2):
                      ht0 = HT * hf
                      consumed = C["vV"]

                      def bcb(e, f=f, ht0=ht0):
                          ins = None
                          for tb in range(2):
                              ins = e.matmul(p_bc[:, 512 * tb:512 * (tb + 1)],
                                             s_indb[32:48, 128 * f:128 * (f + 1)],
                                             B_t[32:48, ht0 + 512 * tb: ht0 + 512 * (tb + 1)],
                                             start=True, stop=True)
                          return ins
                      B.ev("tensor", bcb, waits=(("vV", consumed),), incs=(("tM", 1),))
                      t1 = C["tM"]

                      def bev(e, f=f, ht0=ht0):
                          e.tensor_copy(s_bcb[:, :], p_bc[:, :])
                          return e.tensor_mul(b_buf[:, :],
                                              xpn[:, NT * f + ht0: NT * f + ht0 + HT], s_bcb[:, :])
                      B.ev("vector", bev, waits=(("tM", t1),), incs=(("vV", 1),))
                      bdone = C["vV"]

                      def bcr(e, f=f, ht0=ht0):
                          ins = None
                          for tb in range(2):
                              ins = e.matmul(p_bc[:, 512 * tb:512 * (tb + 1)],
                                             s_inda[0:16, 128 * f:128 * (f + 1)],
                                             RC[:, ht0 + 512 * tb: ht0 + 512 * (tb + 1)],
                                             start=True, stop=True)
                          return ins
                      B.ev("tensor", bcr, waits=(("vV", bdone),), incs=(("tM", 1),))
                      t2 = C["tM"]
                      B.ev("vector", (lambda e: e.tensor_mul(c_buf[:, :], b_buf[:, :], p_bc[:, :])),
                           waits=(("tM", t2),), incs=(("vV", 1),))
                      cdone = C["vV"]

                      def bcc(e, f=f, ht0=ht0):
                          ins = None
                          for tb in range(2):
                              ins = e.matmul(p_bc[:, 512 * tb:512 * (tb + 1)],
                                             s_inda[0:16, 128 * f:128 * (f + 1)],
                                             CU[:, ht0 + 512 * tb: ht0 + 512 * (tb + 1)],
                                             start=True, stop=True)
                          return ins
                      B.ev("tensor", bcc, waits=(("vV", cdone),), incs=(("tM", 1),))
                      t3 = C["tM"]

                      def chain(e, f=f, hf=hf, ht0=ht0):
                          e.tensor_tensor_scan(S_buf[:, :], mask16[:, :], c_buf[:, :], 0.0,
                                               OP.mult, OP.add)
                          e.tensor_single_scalar(t_buf[:, 0::16], S_buf[:, 0::16], 0.0, OP.mult)
                          for p in range(1, 16):
                              e.tensor_mul(t_buf[:, p::16], p_bc[:, p::16], S_buf[:, p - 1::16])
                          e.tensor_add(b_buf[:, :], b_buf[:, :], t_buf[:, :])
                          e.tensor_tensor_scan(carry[:, 64 * hf + 1: 64 * hf + 65],
                                               p_bc[:, 15::16], b_buf[:, 15::16],
                                               carry[:, 64 * hf:64 * hf + 1],
                                               OP.mult, OP.add)
                          for p in range(16):
                              e.tensor_mul(t_buf[:, p::16], p_bc[:, p::16],
                                           carry[:, 64 * hf:64 * hf + 64])
                          return e.tensor_add(HALL[:, NT * f + ht0: NT * f + ht0 + HT],
                                              b_buf[:, :], t_buf[:, :])
                      B.ev("vector", chain, waits=(("tM", t3),), incs=(("vV", 1),))
          B.barrier()
          b4s.close()

          bw_out = A(nc.sbuf_tensor("bw_out", [128, 8 * 1024], BF16))
          bwo_loaded = gload(bw_out[:, :].rearrange("p (c m) -> p c m", c=8),
                             d["bwout"].rearrange("(c p) m -> p c m", p=128))
          with contextlib.ExitStack() as ps:
              p_o = [ps.enter_context(nc.psum_tensor(f"ps{uid()}", [128, 512], F32)) for _ in range(2)]
              it = 0
              oevd = {}
              for oc in range(NF):
                  for tb in range(4):
                      par = it % 2
                      w = [("dG", bwo_loaded)]
                      if par in oevd:
                          w.append(("vV", oevd[par]))

                      def omm(e, oc=oc, tb=tb, par=par):
                          ins = None
                          for ic in range(NF):
                              ins = e.matmul(p_o[par][:, :],
                                             bw_out[:, 1024 * ic + 128 * oc: 1024 * ic + 128 * (oc + 1)],
                                             HALL[:, NT * ic + 512 * tb: NT * ic + 512 * (tb + 1)],
                                             start=(ic == 0), stop=(ic == NF - 1))
                          return ins
                      B.ev("tensor", omm, waits=tuple(w), incs=(("tM", 1),))
                      t4 = C["tM"]

                      def oev_(e, oc=oc, tb=tb, par=par):
                          sl = XS[:, NT * oc + 512 * tb: NT * oc + 512 * (tb + 1)]
                          return e.tensor_add(sl, sl, p_o[par][:, :])
                      B.ev("vector", oev_, waits=(("tM", t4),), incs=(("vV", 1),))
                      oevd[par] = C["vV"]
                      it += 1
          B.barrier()


    if _KSTOP not in ("attn", "ffn1", "attnb", "qonly"):
        emit_ffn()

    for f in range(NF):
        B.ev("sync", dma_fn(d["y"][128 * f:128 * (f + 1), :], XS[:, NT * f:NT * (f + 1)]),
             incs=(("dS", 16),))
    B.ev("sync", None, waits=(("dS", C["dS"]),))

    with nc.Block() as blk:
        B.emit(blk)
    stack.close()
    return nc


_NC_CACHE = [None]


def kernel(**inputs):
    f32 = np.float32
    x = np.asarray(inputs["L0_attn_a_x"], f32)

    def pad32(W):
        Wp = np.zeros((1024, 512), f32)
        for h in range(16):
            Wp[:, 32 * h:32 * h + 16] = W[:, 16 * h:16 * h + 16]
        return Wp

    def pad32b(b):
        bp = np.zeros((1, 512), f32)
        for h in range(16):
            bp[0, 32 * h:32 * h + 16] = b[16 * h:16 * h + 16]
        return bp

    I = {k: np.asarray(v, f32) for k, v in inputs.items()}
    com = {
        "awq": pad32(I["L0_attn_a_Wq"]), "awqc": pad32(I["L0_attn_a_Wqc"]),
        "awk": pad32(I["L0_attn_a_Wk"]), "awkc": pad32(I["L0_attn_a_Wkc"]),
        "abq": pad32b(I["L0_attn_a_bq"]), "abqc": pad32b(I["L0_attn_a_bqc"]),
        "abk": pad32b(I["L0_attn_a_bk"]), "abkc": pad32b(I["L0_attn_a_bkc"]),
        "awv": np.ascontiguousarray(I["L0_attn_a_Wv"]),
        "awvc": np.ascontiguousarray(I["L0_attn_a_Wvc"]),
        "abv": I["L0_attn_a_bv"].reshape(1, 1024),
        "abvc": I["L0_attn_a_bvc"].reshape(1, 1024),
        "bwin": np.ascontiguousarray(I["L0_attn_b_W_in"].T),
        "bwout": np.ascontiguousarray(I["L0_attn_b_W_out"].T),
        "beig": I["L0_attn_b_eig_raw"].reshape(16, 1),
        "bgam": np.ascontiguousarray(I["L0_attn_b_ln_g"].reshape(8, 128).T),
        "bbet": np.ascontiguousarray(I["L0_attn_b_ln_b"].reshape(8, 128).T),
        "fwg": np.ascontiguousarray(I["L0_ffn_Wg"]),
        "fwu": np.ascontiguousarray(I["L0_ffn_Wu"]),
        "fwd": np.ascontiguousarray(
            I["L0_ffn_Wd"].reshape(8, 32, 128, 8, 128).transpose(0, 3, 1, 2, 4)),
    }
    wg = np.zeros((1024, 64), f32)
    wg[:, 0:16] = I["L0_attn_b_W_gate"][:16].T
    wg[:, 32:48] = I["L0_attn_b_W_gate"][16:].T
    bg = np.zeros((1, 64), f32)
    bg[0, 0:16] = I["L0_attn_b_b_gate"][:16]
    bg[0, 32:48] = I["L0_attn_b_b_gate"][16:]
    com["bwgate"] = wg
    com["bbgate"] = bg
    inda = np.zeros((16, 1024), f32)
    indb = np.zeros((48, 1024), f32)
    for f in range(8):
        inda[2 * f, 128 * f:128 * f + 64] = 1.0
        inda[2 * f + 1, 128 * f + 64:128 * f + 128] = 1.0
        indb[32 + 2 * f, 128 * f:128 * f + 64] = 1.0
        indb[32 + 2 * f + 1, 128 * f + 64:128 * f + 128] = 1.0
    com["inda"] = inda
    com["indb"] = indb

    if _NC_CACHE[0] is None:
        _NC_CACHE[0] = build_program()
    nc = _NC_CACHE[0]

    in_maps = []
    for b in range(8):
        m = dict(com)
        m["x"] = np.ascontiguousarray(x[b].T)
        in_maps.append(m)

    res = run_bass_kernel_spmd(nc, in_maps, core_ids=list(range(8)))
    out = np.stack([np.ascontiguousarray(r["y"].T) for r in res.results])
    return out.astype(np.float32)


# version 09
# speedup vs baseline: 1.1136x; 1.1136x over previous
import contextlib
import os
import numpy as np
import ml_dtypes
import concourse.bass as bass
import concourse.mybir as mybir
from concourse.bass_utils import run_bass_kernel_spmd

F32 = mybir.dt.float32
BF16 = mybir.dt.bfloat16
OP = mybir.AluOpType
ACT = mybir.ActivationFunctionType

NT = 2048
SP = 2064
NF = 8
NKC = 17
NH = 16
ENGS = ("sync", "gpsimd", "tensor", "scalar", "vector")

_UID = [0]


def uid():
    _UID[0] += 1
    return _UID[0]


class Ev:
    __slots__ = ("eng", "waits", "fn", "incs")

    def __init__(self, eng, waits, fn, incs):
        self.eng, self.waits, self.fn, self.incs = eng, waits, fn, incs


class Builder:
    def __init__(self, nc, sems):
        self.nc = nc
        self.sems = sems
        self.C = {k: 0 for k in sems}
        self.EV = []
        self.nbar = 0

    def ev(self, eng, fn, waits=(), incs=()):
        selfsem = {"tensor": "tM", "vector": "vV", "scalar": "sS"}.get(eng)
        waits = tuple((s, v) for (s, v) in waits if s != selfsem and v > 0)
        e = Ev(eng, waits, fn, tuple(incs))
        self.EV.append(e)
        for s, k in incs:
            self.C[s] += k
        if fn is not None and eng in ("tensor", "scalar", "vector"):
            if not hasattr(self, "last_real"):
                self.last_real = {}
            self.last_real[eng] = e

    def barrier(self):
        own = {"tensor": "tM", "scalar": "sS", "vector": "vV"}
        lr = getattr(self, "last_real", {})
        for e, sem in own.items():
            ev2 = lr.get(e)
            if ev2 is not None and not ev2.incs:
                ev2.incs = ((sem, 1),)
                self.C[sem] += 1
        snap = dict(self.C)
        for e in ENGS:
            waits = tuple((s, snap[s]) for s in ("dG", "dS", "tM", "vV", "sS"))
            self.ev(e, None, waits=waits)
        self.last_real = {}

    def emit(self, blk):
        sems = self.sems
        for eng_name in ENGS:
            evs = [e for e in self.EV if e.eng == eng_name]

            def body(eng, evs=evs):
                last = {}
                for ev in evs:
                    for (s, v) in ev.waits:
                        if last.get(s, -1) < v:
                            eng.wait_ge(sems[s], v)
                            last[s] = v
                    if ev.fn is None:
                        ins = eng.nop()
                    else:
                        ins = ev.fn(eng)
                    for (s, k) in ev.incs:
                        ins.then_inc(sems[s], k)

            getattr(blk, eng_name)(body)


def build_program():
    nc = bass.Bass("TRN2", target_bir_lowering=False, detect_race_conditions=False)

    d = {}

    def din(name, shape, dt=F32):
        d[name] = nc.dram_tensor(name, shape, dt, kind="ExternalInput").ap()
    din("x", [1024, NT])
    d["y"] = nc.dram_tensor("y", [1024, NT], F32, kind="ExternalOutput").ap()
    for n in ("awq", "awqc", "awk", "awkc"):
        din(n, [1024, 512])
    for n in ("awv", "awvc"):
        din(n, [1024, 1024])
    for n in ("abq", "abqc", "abk", "abkc"):
        din(n, [1, 512])
    for n in ("abv", "abvc"):
        din(n, [1, 1024])
    din("bwin", [1024, 1024])
    din("bwout", [1024, 1024])
    din("bwgate", [1024, 64])
    din("bbgate", [1, 64])
    din("beig", [16, 1])
    din("bgam", [128, 8])
    din("bbet", [128, 8])
    din("inda", [16, 1024])
    din("indb", [48, 1024])
    din("fwg", [8, 1024, 4096])
    din("fwu", [8, 1024, 4096])
    din("fwd", [8, 8, 32, 128, 128])

    stack = contextlib.ExitStack()
    E = stack.enter_context

    sems = {}
    for s in ("PH", "dG", "dS", "tM", "vV", "sS"):
        sems[s] = E(nc.semaphore(s))
    B = Builder(nc, sems)
    C = B.C

    XS = E(nc.sbuf_tensor("XS", [128, NF * NT], F32))
    ones_bf = E(nc.sbuf_tensor("ones_bf", [128, 1], BF16))
    onesr_bf = E(nc.sbuf_tensor("onesr_bf", [1, 512], BF16))
    onesb_f = E(nc.sbuf_tensor("onesb_f", [1, 128], F32))
    s_gam = E(nc.sbuf_tensor("s_gam", [128, 8], F32))
    s_bet = E(nc.sbuf_tensor("s_bet", [128, 8], F32))
    s_eig = E(nc.sbuf_tensor("s_eig", [16, 1], F32))
    s_eigr = E(nc.sbuf_tensor("s_eigr", [16, 1], F32))
    s_bb = {}
    for n, w in (("abq", 512), ("abqc", 512), ("abk", 512), ("abkc", 512),
                 ("abv", 1024), ("abvc", 1024), ("bbgate", 64)):
        s_bb[n] = E(nc.sbuf_tensor("s_" + n, [1, w], BF16))
    s_tmp = E(nc.sbuf_tensor("s_tmp", [128, 512], F32))
    v_mu = E(nc.sbuf_tensor("v_mu", [1, 512], F32))
    v_m2 = E(nc.sbuf_tensor("v_m2", [1, 512], F32))
    v_t1 = E(nc.sbuf_tensor("v_t1", [1, 512], F32))
    v_r = E(nc.sbuf_tensor("v_r", [1, 512], F32))
    v_nm = E(nc.sbuf_tensor("v_nm", [1, 512], F32))

    def dma_fn(dst_ap, src_ap):
        def fn(e, dst=dst_ap, src=src_ap):
            return e.dma_start(dst, src)
        return fn

    for f in range(NF):
        B.ev("sync", dma_fn(XS[:, NT * f:NT * (f + 1)], d["x"][128 * f:128 * (f + 1), :]),
             incs=(("dS", 16),))
    x_loaded = C["dS"]

    def gload(dst_ap, src_ap, waits=()):
        B.ev("gpsimd", dma_fn(dst_ap, src_ap), waits=waits, incs=(("dG", 16),))
        return C["dG"]

    gload(s_gam[:, :], d["bgam"])
    gload(s_bet[:, :], d["bbet"])
    gload(s_eigr[:, :], d["beig"])
    for n in s_bb:
        gload(s_bb[n][:, :], d[n])
    small_loaded = C["dG"]

    B.ev("vector", lambda e: e.memset(ones_bf[:, :], 1.0))
    B.ev("vector", lambda e: e.memset(onesr_bf[:, :], 1.0))
    B.ev("vector", lambda e: e.memset(onesb_f[:, :], 1.0), incs=(("vV", 1),))
    B.ev("scalar", lambda e: e.activation(s_eig[:, :], s_eigr[:, :], ACT.Tanh),
         waits=(("dG", small_loaded),), incs=(("sS", 1),))

    def ln_chain(src_sl, dst_sl, psums, scratch, state, prewaits=(), gamma=False):
        p_s1, p_s2, p_rbc, p_nm = psums
        sxb = scratch
        w = list(prewaits)
        if state.get("stats2"):
            w.append(("tM", state["stats2"]))
        for f in range(NF):
            B.ev("scalar",
                 (lambda e, f=f: e.activation(
                     sxb[:, 512 * f:512 * (f + 1)], src_sl(f), ACT.Copy)),
                 waits=tuple(w) if f == 0 else (),
                 incs=((("sS", 1),) if f == NF - 1 else ()))
        xb_done = C["sS"]

        w = [("sS", xb_done)]
        if state.get("muvar"):
            w.append(("vV", state["muvar"]))

        def stats1(e):
            for f in range(NF):
                ins = e.matmul(p_s1[:, :], ones_bf[:, :], sxb[:, 512 * f:512 * (f + 1)],
                               start=(f == 0), stop=(f == NF - 1))
            return ins
        B.ev("tensor", stats1, waits=tuple(w), incs=(("tM", 1),))
        state["stats1"] = C["tM"]

        for f in range(NF):
            B.ev("scalar",
                 (lambda e, f=f: e.activation(
                     sxb[:, 512 * f:512 * (f + 1)], src_sl(f), ACT.Square)),
                 waits=(("tM", state["stats1"]),) if f == 0 else (),
                 incs=((("sS", 1),) if f == NF - 1 else ()))
        xq_done = C["sS"]

        def stats2(e):
            for f in range(NF):
                ins = e.matmul(p_s2[:, :], ones_bf[:, :], sxb[:, 512 * f:512 * (f + 1)],
                               start=(f == 0), stop=(f == NF - 1))
            return ins
        B.ev("tensor", stats2, waits=(("sS", xq_done),), incs=(("tM", 1),))
        state["stats2"] = C["tM"]

        def muvar(e):
            e.tensor_single_scalar(v_mu[:, :], p_s1[:, :], 1.0 / 1024.0, OP.mult)
            e.tensor_single_scalar(v_m2[:, :], p_s2[:, :], 1.0 / 1024.0, OP.mult)
            e.tensor_mul(v_t1[:, :], v_mu[:, :], v_mu[:, :])
            e.tensor_sub(v_m2[:, :], v_m2[:, :], v_t1[:, :])
            return e.tensor_single_scalar(v_t1[:, :], v_m2[:, :], 1e-5, OP.add)
        B.ev("vector", muvar, waits=(("tM", state["stats2"]),), incs=(("vV", 1),))
        state["muvar"] = C["vV"]

        def rstd(e):
            e.activation(v_m2[:, :], v_t1[:, :], ACT.Ln)
            return e.activation(v_r[:, :], v_m2[:, :], ACT.Exp, scale=-0.5)
        B.ev("scalar", rstd, waits=(("vV", state["muvar"]),), incs=(("sS", 1),))
        rstd_done = C["sS"]

        B.ev("vector", lambda e: e.tensor_mul(v_nm[:, :], v_mu[:, :], v_r[:, :]),
             waits=(("sS", rstd_done),), incs=(("vV", 1),))
        nm_done = C["vV"]

        def bcast(e):
            e.matmul(p_rbc[:, :], onesb_f[:, :], v_r[:, :], start=True, stop=True)
            return e.matmul(p_nm[:, :], onesb_f[:, :], v_nm[:, :], start=True, stop=True)
        B.ev("tensor", bcast, waits=(("vV", nm_done),), incs=(("tM", 1),))
        bcast_done = C["tM"]

        for f in range(NF):
            if not gamma:
                def apply(e, f=f):
                    e.tensor_mul(s_tmp[:, :], src_sl(f), p_rbc[:, :])
                    return e.tensor_sub(dst_sl(f), s_tmp[:, :], p_nm[:, :])
            else:
                def apply(e, f=f):
                    e.tensor_mul(s_tmp[:, :], src_sl(f), p_rbc[:, :])
                    e.tensor_sub(s_tmp[:, :], s_tmp[:, :], p_nm[:, :])
                    return e.tensor_scalar(dst_sl(f), s_tmp[:, :],
                                           s_gam[:, f:f + 1], s_bet[:, f:f + 1],
                                           OP.mult, OP.add)
            B.ev("vector", apply,
                 waits=(("tM", bcast_done),) if f == 0 else (),
                 incs=((("vV", 1),) if f == NF - 1 else ()))
        return C["vV"]

    def mk_ln_psums(ps):
        return (ps.enter_context(nc.psum_tensor(f"ps{uid()}", [1, 512], F32)),
                ps.enter_context(nc.psum_tensor(f"ps{uid()}", [1, 512], F32)),
                ps.enter_context(nc.psum_tensor(f"ps{uid()}", [128, 512], F32)),
                ps.enter_context(nc.psum_tensor(f"ps{uid()}", [128, 512], F32)))

    with contextlib.ExitStack() as attn_scope:
        A = attn_scope.enter_context
        VT = A(nc.sbuf_tensor("VT", [128, NKC * 1040], BF16))
        QF = A(nc.sbuf_tensor("QF", [128, 4 * SP], BF16))
        KF = A(nc.sbuf_tensor("KF", [128, 4 * SP], BF16))
        B.ev("vector", lambda e: e.memset(VT[:, :], 1.0))
        xln_scope = contextlib.ExitStack()
        XLN = xln_scope.enter_context(nc.sbuf_tensor("XLN", [128, NF * SP], BF16))

        for f in range(NF):
            B.ev("vector",
                 (lambda e, f=f: e.memset(XLN[:, SP * f + NT:SP * (f + 1)], 0.0)),
                 incs=((("vV", 1),) if f == NF - 1 else ()))

        with contextlib.ExitStack() as ps:
            psums = mk_ln_psums(ps)
            sxb = ps.enter_context(nc.sbuf_tensor(f"sxb{uid()}", [128, 4096], BF16))
            B.barrier()
            st = {}
            for tb in range(4):
                ln_chain(lambda f, tb=tb: XS[:, NT * f + 512 * tb: NT * f + 512 * (tb + 1)],
                         lambda f, tb=tb: XLN[:, SP * f + 512 * tb: SP * f + 512 * (tb + 1)],
                         psums, sxb, st,
                         prewaits=(("dS", x_loaded),) if tb == 0 else ())
        B.barrier()

        for nb in range(2):
            with contextlib.ExitStack() as p2a:
                aw_v = p2a.enter_context(nc.sbuf_tensor(f"awv{uid()}", [128, 8 * 512], BF16))
                aw_vc = p2a.enter_context(nc.sbuf_tensor(f"awvc{uid()}", [128, 8 * 512], BF16))
                s_sig = p2a.enter_context(nc.sbuf_tensor(f"ssig{uid()}", [128, 512], F32))
                p_v = [p2a.enter_context(nc.psum_tensor(f"ps{uid()}", [128, 512], F32)) for _ in range(2)]
                p_vc = [p2a.enter_context(nc.psum_tensor(f"ps{uid()}", [128, 512], F32)) for _ in range(2)]

                gload(aw_v[:, :].rearrange("p (c m) -> p c m", c=8),
                      d["awv"][:, 512 * nb:512 * (nb + 1)].rearrange("(c p) m -> p c m", p=128))
                awv_loaded = gload(aw_vc[:, :].rearrange("p (c m) -> p c m", c=8),
                                   d["awvc"][:, 512 * nb:512 * (nb + 1)].rearrange("(c p) m -> p c m", p=128))

                evac_done = {}
                for tt in range(NKC):
                    rows = 16 if tt == 16 else 128
                    par = tt % 2
                    w = [("dG", awv_loaded)]
                    if par in evac_done:
                        w.append(("vV", evac_done[par]))

                    def vmms(e, tt=tt, rows=rows, par=par, nb=nb, aw_v=aw_v, aw_vc=aw_vc, p_v=p_v, p_vc=p_vc):
                        tok0 = 128 * tt
                        ins = None
                        for ps_, aw_, ab_ in ((p_v[par], aw_v, s_bb["abv"]),
                                              (p_vc[par], aw_vc, s_bb["abvc"])):
                            for f in range(NF):
                                e.matmul(ps_[0:rows, :],
                                         XLN[:, SP * f + tok0: SP * f + tok0 + rows],
                                         aw_[:, 512 * f:512 * (f + 1)],
                                         start=(f == 0), stop=False)
                            ins = e.matmul(ps_[0:rows, :],
                                           onesr_bf[0:1, 0:rows],
                                           ab_[0:1, 512 * nb:512 * (nb + 1)],
                                           start=False, stop=True)
                        return ins
                    B.ev("tensor", vmms, waits=tuple(w), incs=(("tM", 1),))
                    mm_done = C["tM"]

                    _w_sig = [("tM", mm_done)]
                    if evac_done:
                        _w_sig.append(("vV", max(evac_done.values())))
                    B.ev("scalar",
                         (lambda e, rows=rows, par=par, s_sig=s_sig, p_vc=p_vc: e.activation(
                             s_sig[0:rows, :], p_vc[par][0:rows, :], ACT.Sigmoid)),
                         waits=tuple(_w_sig), incs=(("sS", 1),))
                    sg_done = C["sS"]

                    B.ev("vector",
                         (lambda e, tt=tt, rows=rows, par=par, nb=nb, p_v=p_v, s_sig=s_sig: e.tensor_mul(
                             VT[0:rows, 1040 * tt + 520 * nb:1040 * tt + 520 * nb + 520]
                             .rearrange("p (h c) -> p h c", c=65)[:, :, 0:64],
                             p_v[par][0:rows, :].rearrange("p (h c) -> p h c", c=64),
                             s_sig[0:rows, :].rearrange("p (h c) -> p h c", c=64))),
                         waits=(("sS", sg_done),), incs=(("vV", 1),))
                    evac_done[par] = C["vV"]
            B.barrier()

        for dst, wname, wcname, bname, bcname in (
                (QF, "awq", "awqc", "abq", "abqc"),
                (KF, "awk", "awkc", "abk", "abkc")):
            with contextlib.ExitStack() as p2b:
                awW = p2b.enter_context(nc.sbuf_tensor(f"aw{uid()}", [128, 8 * 512], BF16))
                awC = p2b.enter_context(nc.sbuf_tensor(f"aw{uid()}", [128, 8 * 512], BF16))
                gload(awW[:, :].rearrange("p (c m) -> p c m", c=8),
                      d[wname].rearrange("(c p) m -> p c m", p=128))
                qk_loaded = gload(awC[:, :].rearrange("p (c m) -> p c m", c=8),
                                  d[wcname].rearrange("(c p) m -> p c m", p=128))
                s_sig2 = p2b.enter_context(nc.sbuf_tensor(f"sg2{uid()}", [128, 512], F32))
                p_q = [p2b.enter_context(nc.psum_tensor(f"ps{uid()}", [128, 512], F32)) for _ in range(2)]
                p_qc = [p2b.enter_context(nc.psum_tensor(f"ps{uid()}", [128, 512], F32)) for _ in range(2)]

                evac_done = {}
                it = 0
                for oc in range(4):
                    for tb in range(5):
                        ncols = 16 if tb == 4 else 512
                        tok0 = 512 * tb
                        par = it % 2
                        w = [("dG", qk_loaded)]
                        if par in evac_done:
                            w.append(("vV", evac_done[par]))

                        def qmms(e, oc=oc, tok0=tok0, ncols=ncols, par=par, awW=awW, awC=awC, p_q=p_q, p_qc=p_qc, bname=bname, bcname=bcname):
                            ins = None
                            for ps_, aw_, ab_n in ((p_q[par], awW, bname),
                                                   (p_qc[par], awC, bcname)):
                                for f in range(NF):
                                    e.matmul(ps_[:, 0:ncols],
                                             aw_[:, 512 * f + 128 * oc: 512 * f + 128 * (oc + 1)],
                                             XLN[:, SP * f + tok0: SP * f + tok0 + ncols],
                                             start=(f == 0), stop=False)
                                ins = e.matmul(ps_[:, 0:ncols],
                                               s_bb[ab_n][0:1, 128 * oc:128 * (oc + 1)],
                                               onesr_bf[0:1, 0:ncols],
                                               start=False, stop=True)
                            return ins
                        B.ev("tensor", qmms, waits=tuple(w), incs=(("tM", 1),))
                        mm_done = C["tM"]

                        _w_sig = [("tM", mm_done)]
                        if evac_done:
                            _w_sig.append(("vV", max(evac_done.values())))
                        B.ev("scalar",
                             (lambda e, ncols=ncols, par=par, s_sig2=s_sig2, p_qc=p_qc: e.activation(
                                 s_sig2[:, 0:ncols], p_qc[par][:, 0:ncols], ACT.Sigmoid)),
                             waits=tuple(_w_sig), incs=(("sS", 1),))
                        sg_done = C["sS"]

                        B.ev("vector",
                             (lambda e, dst=dst, oc=oc, tok0=tok0, ncols=ncols, par=par,
                                     p_q=p_q, s_sig2=s_sig2:
                              e.tensor_mul(dst[:, SP * oc + tok0: SP * oc + tok0 + ncols],
                                           p_q[par][:, 0:ncols], s_sig2[:, 0:ncols])),
                             waits=(("sS", sg_done),), incs=(("vV", 1),))
                        evac_done[par] = C["vV"]
                        it += 1
            B.barrier()

        xln_scope.close()

        if os.environ.get("KSTOP", "") == "qonly":
            _p3_skip = True
        else:
            _p3_skip = False
        with contextlib.ExitStack() as p3:
          if not _p3_skip:
              PT = [p3.enter_context(nc.sbuf_tensor(f"PT{i}", [128, NKC * 512], BF16)) for i in range(2)]
              v_rz = p3.enter_context(nc.sbuf_tensor("v_rz", [1, 512], F32))
              s_ct = p3.enter_context(nc.sbuf_tensor("s_ct", [128, 512], F32))
              s_zb = p3.enter_context(nc.sbuf_tensor("s_zb", [128, 512], F32))
              p_st = [p3.enter_context(nc.psum_tensor(f"ps{uid()}", [128, 1024], F32)) for _ in range(2)]
              p_ctx = [p3.enter_context(nc.psum_tensor(f"ps{uid()}", [65, 512], F32)) for _ in range(3)]
              p_zb = p3.enter_context(nc.psum_tensor(f"ps{uid()}", [128, 512], F32))

              exp_done_all = []
              recip_done = {}
              evac_done = {}
              zstop_done = {}
              hdone = {}

              gidx = 0
              for qblk in range(4):
                  for h in range(NH):
                      g = gidx
                      gidx += 1
                      pr = g // 2
                      hc = h // 4
                      hr = 32 * (h % 4)
                      q0 = 512 * qblk
                      rowoff = 64 * (h % 2)

                      for kp in range(9):
                          P = 9 * g + kp
                          w = []
                          if P >= 2:
                              w.append(("sS", exp_done_all[P - 2]))

                          def stmms(e, hc=hc, hr=hr, q0=q0, kp=kp, P=P):
                              ins = None
                              for ki in range(2):
                                  kc = 2 * kp + ki
                                  if kc > 16:
                                      break
                                  rows = 16 if kc == 16 else 128
                                  ins = e.matmul(
                                      p_st[P % 2][0:rows, 512 * ki:512 * ki + 512],
                                      KF[hr:hr + 32, SP * hc + 128 * kc: SP * hc + 128 * kc + rows],
                                      QF[hr:hr + 32, SP * hc + q0: SP * hc + q0 + 512],
                                      start=True, stop=True, tile_position=(hr, 0))
                              return ins
                          B.ev("tensor", stmms, waits=tuple(w), incs=(("tM", 1),))
                          st_done = C["tM"]

                          w = [("tM", st_done)]
                          if g >= 2 and kp == 0:
                              w.append(("tM", zstop_done[g - 2]))

                          def expev(e, g=g, kp=kp, P=P):
                              kc0 = 2 * kp
                              if kc0 == 16:
                                  return e.activation(PT[g % 2][0:16, 512 * 16:512 * 17],
                                                      p_st[P % 2][0:16, 0:512],
                                                      ACT.Exp, scale=0.125)
                              return e.activation(PT[g % 2][:, 512 * kc0:512 * (kc0 + 2)],
                                                  p_st[P % 2][:, :], ACT.Exp, scale=0.125)
                          B.ev("scalar", expev, waits=tuple(w), incs=(("sS", 1),))
                          exp_done_all.append(C["sS"])

                      w = [("sS", exp_done_all[9 * g + 8])]
                      if g >= 3:
                          w.append(("vV", evac_done[(g - 3) // 2]))

                      def zctx(e, g=g, h=h):
                          ins = None
                          for kc in range(NKC):
                              rows = 16 if kc == 16 else 128
                              pt = PT[g % 2][0:rows, 512 * kc:512 * kc + 512]
                              ins = e.matmul(
                                  p_ctx[g % 3][0:65, :],
                                  VT[0:rows, 1040 * kc + 65 * h: 1040 * kc + 65 * h + 65],
                                  pt, start=(kc == 0), stop=(kc == NKC - 1))
                          return ins
                      B.ev("tensor", zctx, waits=tuple(w), incs=(("tM", 1),))
                      zstop_done[g] = C["tM"]

                      B.ev("vector",
                           (lambda e, g=g: e.reciprocal(v_rz[:, :], p_ctx[g % 3][64:65, :])),
                           waits=(("tM", zstop_done[g]),), incs=(("vV", 1),))
                      recip_done[g] = C["vV"]

                      w = [("vV", recip_done[g])]
                      if pr >= 1 and h % 2 == 0:
                          w.append(("vV", evac_done[pr - 1]))

                      def zbev(e, rowoff=rowoff):
                          return e.matmul(p_zb[rowoff:rowoff + 64, :],
                                          onesb_f[0:1, 0:64], v_rz[:, :],
                                          start=True, stop=True)
                      B.ev("tensor", zbev, waits=tuple(w), incs=(("tM", 1),))
                      hdone[g] = C["tM"]

                      if h % 2 == 1:
                          fch = (h - 1) // 2

                          def evac(e, fch=fch, qblk=qblk, g=g):
                              e.tensor_copy(s_zb[:, :], p_zb[:, :])
                              e.tensor_mul(s_ct[0:64, :], p_ctx[(g - 1) % 3][0:64, :], s_zb[0:64, :])
                              e.tensor_mul(s_ct[64:128, :], p_ctx[g % 3][0:64, :], s_zb[64:128, :])
                              sl = XS[:, NT * fch + 512 * qblk: NT * fch + 512 * (qblk + 1)]
                              return e.tensor_add(sl, sl, s_ct[:, :])
                          B.ev("vector", evac, waits=(("tM", hdone[g]),), incs=(("vV", 1),))
                          evac_done[pr] = C["vV"]
        B.barrier()

    _KSTOP = os.environ.get("KSTOP", "")

    def emit_ffn():
        with contextlib.ExitStack() as fs:
            A = fs.enter_context
            XLNf = A(nc.sbuf_tensor(f"XLNf{uid()}", [128, NF * 1024], BF16))
            HB = A(nc.sbuf_tensor(f"HB{uid()}", [128, 32 * 1024], BF16))
            wg_buf = [A(nc.sbuf_tensor(f"wg{uid()}", [128, 8 * 256], BF16)) for _ in range(2)]
            wu_buf = [A(nc.sbuf_tensor(f"wu{uid()}", [128, 8 * 256], BF16)) for _ in range(2)]
            wd_buf = [A(nc.sbuf_tensor(f"wd{uid()}", [128, 16 * 128], BF16)) for _ in range(2)]
            s_sg = A(nc.sbuf_tensor(f"sg{uid()}", [128, 1024], F32))

            free_tm = {}
            free_vv = {}
            free_ss = {}
            dma_done = {}

            for half in range(2):
                with contextlib.ExitStack() as ps:
                    psums = mk_ln_psums(ps)
                    sxb = ps.enter_context(nc.sbuf_tensor(f"sxb{uid()}", [128, 4096], BF16))
                    B.barrier()
                    st = {}
                    for tb in range(2):
                        ln_chain(
                            lambda f, tb=tb, half=half: XS[:, NT * f + 1024 * half + 512 * tb:
                                                           NT * f + 1024 * half + 512 * (tb + 1)],
                            lambda f, tb=tb: XLNf[:, 1024 * f + 512 * tb: 1024 * f + 512 * (tb + 1)],
                            psums, sxb, st)
                B.barrier()

                with contextlib.ExitStack() as ps:
                    p_g = ps.enter_context(nc.psum_tensor(f"ps{uid()}", [128, 1024], F32))
                    p_u = ps.enter_context(nc.psum_tensor(f"ps{uid()}", [128, 1024], F32))
                    p_y = [ps.enter_context(nc.psum_tensor(f"ps{uid()}", [128, 1024], F32))
                           for _ in range(2)]

                    for e_ in range(8):
                        def dma_wgwu(j16, e_=e_):
                            for nm_, dd, buf in (("wg", d["fwg"], wg_buf), ("wu", d["fwu"], wu_buf)):
                                w = []
                                if (nm_, j16 % 2) in free_tm:
                                    w.append(("tM", free_tm[(nm_, j16 % 2)]))

                                def fn(e, dd=dd, e_=e_, j16=j16, buf=buf):
                                    return e.dma_start(
                                        buf[j16 % 2][:, :].rearrange("p (c m) -> p c m", c=8),
                                        dd[e_][:, 256 * j16:256 * (j16 + 1)].rearrange(
                                            "(c p) m -> p c m", p=128))
                                B.ev("gpsimd", fn, waits=tuple(w), incs=(("dG", 16),))
                                dma_done[(nm_, j16 % 2)] = C["dG"]

                        def dma_wd(ocmh, e_=e_):
                            oc, mh = ocmh // 2, ocmh % 2
                            w = []
                            if ("wd", ocmh % 2) in free_tm:
                                w.append(("tM", free_tm[("wd", ocmh % 2)]))

                            def fn(e, e_=e_, oc=oc, mh=mh, ocmh=ocmh):
                                return e.dma_start(
                                    wd_buf[ocmh % 2][:, :].rearrange("p (c m) -> p c m", c=16),
                                    d["fwd"][e_][oc][16 * mh:16 * (mh + 1)].transpose([1, 0, 2]))
                            B.ev("gpsimd", fn, waits=tuple(w), incs=(("dG", 16),))
                            dma_done[("wd", ocmh % 2)] = C["dG"]

                        dma_wgwu(0)
                        dma_wgwu(1)
                        dma_wd(0)
                        dma_wd(1)

                        for m in range(32):
                            j16 = m // 2
                            mo = m % 2
                            w = [("dG", dma_done[("wu", j16 % 2)])]
                            if ("pg",) in free_ss:
                                w.append(("sS", free_ss[("pg",)]))

                            def gmms(e, j16=j16, mo=mo, p_g=p_g):
                                ins = None
                                for f in range(NF):
                                    for tb in range(2):
                                        ins = e.matmul(
                                            p_g[:, 512 * tb:512 * (tb + 1)],
                                            wg_buf[j16 % 2][:, 256 * f + 128 * mo: 256 * f + 128 * (mo + 1)],
                                            XLNf[:, 1024 * f + 512 * tb: 1024 * f + 512 * (tb + 1)],
                                            start=(f == 0), stop=(f == NF - 1))
                                return ins
                            B.ev("tensor", gmms, waits=tuple(w), incs=(("tM", 1),))
                            g_done = C["tM"]

                            w = []
                            if ("pu",) in free_vv:
                                w.append(("vV", free_vv[("pu",)]))

                            def umms(e, j16=j16, mo=mo, p_u=p_u):
                                ins = None
                                for f in range(NF):
                                    for tb in range(2):
                                        ins = e.matmul(
                                            p_u[:, 512 * tb:512 * (tb + 1)],
                                            wu_buf[j16 % 2][:, 256 * f + 128 * mo: 256 * f + 128 * (mo + 1)],
                                            XLNf[:, 1024 * f + 512 * tb: 1024 * f + 512 * (tb + 1)],
                                            start=(f == 0), stop=(f == NF - 1))
                                return ins
                            B.ev("tensor", umms, waits=tuple(w), incs=(("tM", 1),))
                            u_done = C["tM"]
                            if mo == 1:
                                free_tm[("wg", j16 % 2)] = g_done
                                free_tm[("wu", j16 % 2)] = u_done
                                if j16 + 2 < 16:
                                    dma_wgwu(j16 + 2)

                            w = [("tM", g_done)]
                            if ("sg",) in free_vv:
                                w.append(("vV", free_vv[("sg",)]))
                            B.ev("scalar",
                                 (lambda e, p_g=p_g: e.activation(s_sg[:, :], p_g[:, :], ACT.Silu)),
                                 waits=tuple(w), incs=(("sS", 1),))
                            free_ss[("pg",)] = C["sS"]

                            w = [("sS", free_ss[("pg",)]), ("tM", u_done)]
                            if ("HB",) in free_tm:
                                w.append(("tM", free_tm[("HB",)]))

                            B.ev("vector",
                                 (lambda e, m=m, p_u=p_u: e.tensor_mul(
                                     HB[:, 1024 * m:1024 * (m + 1)], s_sg[:, :], p_u[:, :])),
                                 waits=tuple(w), incs=(("vV", 1),))
                            free_vv[("pu",)] = C["vV"]
                            free_vv[("sg",)] = C["vV"]
                        hb_ready = C["vV"]
                        free_tm.pop(("HB",), None)

                        for oc in range(8):
                            for mh in range(2):
                                ocmh = oc * 2 + mh
                                w = [("dG", dma_done[("wd", ocmh % 2)])]
                                if mh == 0:
                                    w.append(("vV", hb_ready))
                                    if ("py", oc % 2) in free_vv:
                                        w.append(("vV", free_vv[("py", oc % 2)]))

                                def ymms(e, oc=oc, mh=mh, ocmh=ocmh, p_y=p_y):
                                    ins = None
                                    for mm_ in range(16):
                                        m = 16 * mh + mm_
                                        for tb in range(2):
                                            ins = e.matmul(
                                                p_y[oc % 2][:, 512 * tb:512 * (tb + 1)],
                                                wd_buf[ocmh % 2][:, 128 * mm_:128 * (mm_ + 1)],
                                                HB[:, 1024 * m + 512 * tb: 1024 * m + 512 * (tb + 1)],
                                                start=(m == 0), stop=(m == 31))
                                    return ins
                                B.ev("tensor", ymms, waits=tuple(w), incs=(("tM", 1),))
                                free_tm[("wd", ocmh % 2)] = C["tM"]
                                y_done = C["tM"]
                                if ocmh + 2 < 16:
                                    dma_wd(ocmh + 2)
                            if oc == 7:
                                free_tm[("HB",)] = C["tM"]

                            def yev(e, oc=oc, half=half, p_y=p_y):
                                sl = XS[:, NT * oc + 1024 * half: NT * oc + 1024 * (half + 1)]
                                return e.tensor_add(sl, sl, p_y[oc % 2][:, :])
                            B.ev("vector", yev, waits=(("tM", y_done),), incs=(("vV", 1),))
                            free_vv[("py", oc % 2)] = C["vV"]
                B.barrier()

    if _KSTOP not in ("attn", "qonly"):
        emit_ffn()

    if _KSTOP not in ("attn", "ffn1", "qonly"):
      with contextlib.ExitStack() as bs:
          A = bs.enter_context
          xpn = A(nc.sbuf_tensor("xpn", [128, NF * NT], BF16))
          B_t = A(nc.sbuf_tensor("B_t", [48, NT], F32))
          CU = A(nc.sbuf_tensor("CU", [16, NT], F32))
          RC = A(nc.sbuf_tensor("RC", [16, NT], F32))
          s_inda = A(nc.sbuf_tensor("s_inda", [16, 1024], F32))
          s_indb = A(nc.sbuf_tensor("s_indb", [48, 1024], F32))
          gload(s_inda[:, :], d["inda"])
          ind_loaded = gload(s_indb[:, :], d["indb"])

          a_scope = contextlib.ExitStack()
          A_t = a_scope.enter_context(nc.sbuf_tensor("A_t", [16, NT], F32))

          with contextlib.ExitStack() as b2s:
              A2 = b2s.enter_context
              XLNb = A2(nc.sbuf_tensor("XLNb", [128, 8 * 512], BF16))
              bw_in = A2(nc.sbuf_tensor("bw_in", [128, 8 * 1024], BF16))
              bw_gate = A2(nc.sbuf_tensor("bw_gate", [128, 8 * 64], BF16))
              XP = A2(nc.sbuf_tensor("XP", [128, 8 * 512], F32))
              sxb = A2(nc.sbuf_tensor(f"sxb{uid()}", [128, 4096], BF16))

              gload(bw_in[:, :].rearrange("p (c m) -> p c m", c=8),
                    d["bwin"].rearrange("(c p) m -> p c m", p=128))
              bw_loaded = gload(bw_gate[:, :].rearrange("p (c m) -> p c m", c=8),
                                d["bwgate"].rearrange("(c p) m -> p c m", p=128))

              with contextlib.ExitStack() as ps:
                  p_xp = [ps.enter_context(nc.psum_tensor(f"ps{uid()}", [128, 512], F32)) for _ in range(2)]
                  p_gt = ps.enter_context(nc.psum_tensor(f"ps{uid()}", [64, 512], F32))
                  psums = mk_ln_psums(ps)
                  B.barrier()

                  lnst = {}
                  free_xp = {}
                  xpmm_done = {}
                  sigg_done = 0
                  for tb in range(4):
                      t0 = 512 * tb
                      pw = []
                      if tb == 0:
                          pw.append(("dG", bw_loaded))
                      if tb >= 1:
                          pw.append(("tM", xpmm_done[tb - 1]))
                      apply3_done = ln_chain(
                          lambda f, t0=t0: XS[:, NT * f + t0: NT * f + t0 + 512],
                          lambda f: XLNb[:, 512 * f:512 * (f + 1)],
                          psums, sxb, lnst, prewaits=tuple(pw))

                      for oc in range(8):
                          slot = oc % 2
                          w = [("vV", apply3_done)]
                          if slot in free_xp:
                              w.append(("vV", free_xp[slot][0]))
                              w.append(("sS", free_xp[slot][1]))

                          def xpmm(e, oc=oc, slot=slot):
                              ins = None
                              for f in range(NF):
                                  ins = e.matmul(p_xp[slot][:, :],
                                                 bw_in[:, 1024 * f + 128 * oc: 1024 * f + 128 * (oc + 1)],
                                                 XLNb[:, 512 * f:512 * (f + 1)],
                                                 start=(f == 0), stop=(f == NF - 1))
                              return ins
                          B.ev("tensor", xpmm, waits=tuple(w), incs=(("tM", 1),))
                          mmv = C["tM"]

                          def xev(e, oc=oc, slot=slot):
                              return e.tensor_copy(XP[:, 512 * oc:512 * (oc + 1)], p_xp[slot][:, :])
                          B.ev("vector", xev, waits=(("tM", mmv),), incs=(("vV", 1),))
                          vvv = C["vV"]

                          def xbev(e, oc=oc, slot=slot):
                              return e.activation(sxb[:, 512 * oc:512 * (oc + 1)], p_xp[slot][:, :],
                                                  ACT.Copy)
                          B.ev("scalar", xbev, waits=(("tM", mmv), ("vV", vvv)),
                             incs=(("sS", 1),))
                          free_xp[slot] = (vvv, C["sS"])
                      xpmm_done[tb] = C["tM"]
                      sS_evacs = C["sS"]
                      vV_evacs = C["vV"]

                      def stats1(e):
                          for f in range(NF):
                              ins = e.matmul(psums[0][:, :], ones_bf[:, :], sxb[:, 512 * f:512 * (f + 1)],
                                             start=(f == 0), stop=(f == NF - 1))
                          return ins
                      B.ev("tensor", stats1, waits=(("sS", sS_evacs), ("vV", lnst.get("muvar", 0))),
                           incs=(("tM", 1),))
                      lnst["stats1"] = C["tM"]

                      for f in range(NF):
                          B.ev("scalar",
                               (lambda e, f=f: e.activation(sxb[:, 512 * f:512 * (f + 1)],
                                                            XP[:, 512 * f:512 * (f + 1)], ACT.Square)),
                               waits=(("tM", lnst["stats1"]), ("vV", vV_evacs)) if f == 0 else (),
                               incs=((("sS", 1),) if f == NF - 1 else ()))
                      xq_done = C["sS"]

                      def stats2(e):
                          for f in range(NF):
                              ins = e.matmul(psums[1][:, :], ones_bf[:, :], sxb[:, 512 * f:512 * (f + 1)],
                                             start=(f == 0), stop=(f == NF - 1))
                          return ins
                      B.ev("tensor", stats2, waits=(("sS", xq_done),), incs=(("tM", 1),))
                      lnst["stats2"] = C["tM"]

                      def muvar(e):
                          e.tensor_single_scalar(v_mu[:, :], psums[0][:, :], 1.0 / 1024.0, OP.mult)
                          e.tensor_single_scalar(v_m2[:, :], psums[1][:, :], 1.0 / 1024.0, OP.mult)
                          e.tensor_mul(v_t1[:, :], v_mu[:, :], v_mu[:, :])
                          e.tensor_sub(v_m2[:, :], v_m2[:, :], v_t1[:, :])
                          return e.tensor_single_scalar(v_t1[:, :], v_m2[:, :], 1e-5, OP.add)
                      B.ev("vector", muvar, waits=(("tM", lnst["stats2"]),), incs=(("vV", 1),))
                      lnst["muvar"] = C["vV"]

                      def rstd(e):
                          e.activation(v_m2[:, :], v_t1[:, :], ACT.Ln)
                          return e.activation(v_r[:, :], v_m2[:, :], ACT.Exp, scale=-0.5)
                      B.ev("scalar", rstd, waits=(("vV", lnst["muvar"]),), incs=(("sS", 1),))
                      rstd_done = C["sS"]

                      B.ev("vector", lambda e: e.tensor_mul(v_nm[:, :], v_mu[:, :], v_r[:, :]),
                           waits=(("sS", rstd_done),), incs=(("vV", 1),))
                      nm_done = C["vV"]

                      def bcast(e):
                          e.matmul(psums[2][:, :], onesb_f[:, :], v_r[:, :], start=True, stop=True)
                          return e.matmul(psums[3][:, :], onesb_f[:, :], v_nm[:, :], start=True, stop=True)
                      B.ev("tensor", bcast, waits=(("vV", nm_done),), incs=(("tM", 1),))
                      bcast_done = C["tM"]

                      for oc in range(NF):
                          def apply(e, oc=oc, t0=t0):
                              e.tensor_mul(s_tmp[:, :], XP[:, 512 * oc:512 * (oc + 1)], psums[2][:, :])
                              e.tensor_sub(s_tmp[:, :], s_tmp[:, :], psums[3][:, :])
                              return e.tensor_scalar(xpn[:, NT * oc + t0: NT * oc + t0 + 512],
                                                     s_tmp[:, :], s_gam[:, oc:oc + 1],
                                                     s_bet[:, oc:oc + 1], OP.mult, OP.add)
                          B.ev("vector", apply,
                               waits=(("tM", bcast_done),) if oc == 0 else (),
                               incs=((("vV", 1),) if oc == NF - 1 else ()))
                      apply_done = C["vV"]

                      def gmm(e, t0=t0):
                          for oc in range(NF):
                              e.matmul(p_gt[:, :],
                                       bw_gate[:, 64 * oc:64 * (oc + 1)],
                                       xpn[:, NT * oc + t0: NT * oc + t0 + 512],
                                       start=(oc == 0), stop=False)
                          return e.matmul(p_gt[:, :], s_bb["bbgate"][:, :], onesr_bf[:, 0:512],
                                          start=False, stop=True)
                      w = [("vV", apply_done)]
                      if sigg_done:
                          w.append(("sS", sigg_done))
                      B.ev("tensor", gmm, waits=tuple(w), incs=(("tM", 1),))
                      gmm_done = C["tM"]

                      def sigg(e, t0=t0):
                          e.activation(A_t[0:16, t0:t0 + 512], p_gt[0:16, :], ACT.Sigmoid)
                          return e.activation(B_t[32:48, t0:t0 + 512], p_gt[32:48, :], ACT.Sigmoid)
                      B.ev("scalar", sigg, waits=(("tM", gmm_done), ("dG", ind_loaded)),
                           incs=(("sS", 1),))
                      sigg_done = C["sS"]
              B.barrier()

          with contextlib.ExitStack() as b3s:
              CP_ = b3s.enter_context(nc.sbuf_tensor("CP_", [16, NT], F32))
              G1 = b3s.enter_context(nc.sbuf_tensor("G1", [16, NT], F32))

              def b3(e):
                  e.tensor_single_scalar(CP_[:, :], A_t[0:16, :], s_eig[:, :], OP.mult)
                  e.tensor_copy(CU[:, :], CP_[:, :])
                  for p in range(1, 16):
                      e.tensor_mul(CU[:, p::16], CU[:, p - 1::16], CP_[:, p::16])
                  e.tensor_copy(A_t[0:16, 1:NT], CU[:, 0:NT - 1])
                  e.tensor_scalar(A_t[0:16, 0::16], CU[:, 0::16], 0.0, 1.0, OP.mult, OP.add)
                  e.tensor_single_scalar(G1[:, :], A_t[0:16, :], 1e-8, OP.is_gt)
                  e.tensor_single_scalar(RC[:, :], A_t[0:16, :], -1e-8, OP.is_lt)
                  e.tensor_max(G1[:, :], G1[:, :], RC[:, :])
                  e.tensor_mul(RC[:, :], A_t[0:16, :], G1[:, :])
                  e.tensor_scalar(CP_[:, :], G1[:, :], -1.0, 1.0, OP.mult, OP.add)
                  e.tensor_add(RC[:, :], RC[:, :], CP_[:, :])
                  e.reciprocal(CP_[:, :], RC[:, :])
                  return e.tensor_mul(RC[:, :], CP_[:, :], G1[:, :])
              B.ev("vector", b3, incs=(("vV", 1),))
              B.barrier()
          a_scope.close()

          HT = NT // 2
          HALL = A(nc.sbuf_tensor("HALL", [128, NF * NT], BF16))
          b4s = contextlib.ExitStack()
          A4 = b4s.enter_context
          mask16 = A4(nc.sbuf_tensor("mask16", [128, HT], F32))
          b_buf = A4(nc.sbuf_tensor("b_buf", [128, HT], F32))
          c_buf = A4(nc.sbuf_tensor("c_buf", [128, HT], F32))
          S_buf = A4(nc.sbuf_tensor("S_buf", [128, HT + 16], F32))
          t_buf = A4(nc.sbuf_tensor("t_buf", [128, HT], F32))
          carry = A4(nc.sbuf_tensor("carry", [128, 132], F32))
          s_bcb = A4(nc.sbuf_tensor("s_bcb", [128, HT], BF16))

          def mkmask(e):
              e.memset(mask16[:, :], 1.0)
              e.memset(mask16[:, 0::16], 0.0)
              e.memset(S_buf[:, 0:1], 0.0)
              return e.memset(carry[:, 0:1], 0.0)
          B.ev("vector", mkmask, incs=(("vV", 1),))

          with contextlib.ExitStack() as ps:
              p_bc = ps.enter_context(nc.psum_tensor(f"ps{uid()}", [128, HT], F32))
              for f in range(NF):
                  for hf in range(2):
                      ht0 = HT * hf
                      consumed = C["vV"]

                      def bcb(e, f=f, ht0=ht0):
                          ins = None
                          for tb in range(2):
                              ins = e.matmul(p_bc[:, 512 * tb:512 * (tb + 1)],
                                             s_indb[32:48, 128 * f:128 * (f + 1)],
                                             B_t[32:48, ht0 + 512 * tb: ht0 + 512 * (tb + 1)],
                                             start=True, stop=True)
                          return ins
                      B.ev("tensor", bcb, waits=(("vV", consumed),), incs=(("tM", 1),))
                      t1 = C["tM"]

                      def bev(e, f=f, ht0=ht0):
                          e.tensor_copy(s_bcb[:, :], p_bc[:, :])
                          return e.tensor_mul(b_buf[:, :],
                                              xpn[:, NT * f + ht0: NT * f + ht0 + HT], s_bcb[:, :])
                      B.ev("vector", bev, waits=(("tM", t1),), incs=(("vV", 1),))
                      bdone = C["vV"]

                      def bcr(e, f=f, ht0=ht0):
                          ins = None
                          for tb in range(2):
                              ins = e.matmul(p_bc[:, 512 * tb:512 * (tb + 1)],
                                             s_inda[0:16, 128 * f:128 * (f + 1)],
                                             RC[:, ht0 + 512 * tb: ht0 + 512 * (tb + 1)],
                                             start=True, stop=True)
                          return ins
                      B.ev("tensor", bcr, waits=(("vV", bdone),), incs=(("tM", 1),))
                      t2 = C["tM"]
                      B.ev("vector", (lambda e: e.tensor_mul(c_buf[:, :], b_buf[:, :], p_bc[:, :])),
                           waits=(("tM", t2),), incs=(("vV", 1),))
                      cdone = C["vV"]

                      def bcc(e, f=f, ht0=ht0):
                          ins = None
                          for tb in range(2):
                              ins = e.matmul(p_bc[:, 512 * tb:512 * (tb + 1)],
                                             s_inda[0:16, 128 * f:128 * (f + 1)],
                                             CU[:, ht0 + 512 * tb: ht0 + 512 * (tb + 1)],
                                             start=True, stop=True)
                          return ins
                      B.ev("tensor", bcc, waits=(("vV", cdone),), incs=(("tM", 1),))
                      t3 = C["tM"]

                      def chain(e, f=f, hf=hf, ht0=ht0):
                          e.tensor_tensor_scan(S_buf[:, 1:HT + 1], mask16[:, :], c_buf[:, :],
                                               0.0, OP.mult, OP.add)
                          e.tensor_mul(t_buf[:, :], S_buf[:, 0:HT], mask16[:, :])
                          e.tensor_mul(t_buf[:, :], t_buf[:, :], p_bc[:, :])
                          e.tensor_add(b_buf[:, :], b_buf[:, :], t_buf[:, :])
                          e.tensor_tensor_scan(carry[:, 64 * hf + 1: 64 * hf + 65],
                                               p_bc[:, 15::16], b_buf[:, 15::16],
                                               carry[:, 64 * hf:64 * hf + 1],
                                               OP.mult, OP.add)
                          e.tensor_mul(t_buf[:, :].rearrange("p (n i) -> p n i", i=16),
                                       p_bc[:, :].rearrange("p (n i) -> p n i", i=16),
                                       carry[:, 64 * hf:64 * hf + 64]
                                       .rearrange("p (n one) -> p n one", one=1)
                                       .broadcast_to([128, 64, 16]))
                          return e.tensor_add(HALL[:, NT * f + ht0: NT * f + ht0 + HT],
                                              b_buf[:, :], t_buf[:, :])
                      B.ev("vector", chain, waits=(("tM", t3),), incs=(("vV", 1),))
          B.barrier()
          b4s.close()

          bw_out = A(nc.sbuf_tensor("bw_out", [128, 8 * 1024], BF16))
          bwo_loaded = gload(bw_out[:, :].rearrange("p (c m) -> p c m", c=8),
                             d["bwout"].rearrange("(c p) m -> p c m", p=128))
          with contextlib.ExitStack() as ps:
              p_o = [ps.enter_context(nc.psum_tensor(f"ps{uid()}", [128, 512], F32)) for _ in range(2)]
              it = 0
              oevd = {}
              for oc in range(NF):
                  for tb in range(4):
                      par = it % 2
                      w = [("dG", bwo_loaded)]
                      if par in oevd:
                          w.append(("vV", oevd[par]))

                      def omm(e, oc=oc, tb=tb, par=par):
                          ins = None
                          for ic in range(NF):
                              ins = e.matmul(p_o[par][:, :],
                                             bw_out[:, 1024 * ic + 128 * oc: 1024 * ic + 128 * (oc + 1)],
                                             HALL[:, NT * ic + 512 * tb: NT * ic + 512 * (tb + 1)],
                                             start=(ic == 0), stop=(ic == NF - 1))
                          return ins
                      B.ev("tensor", omm, waits=tuple(w), incs=(("tM", 1),))
                      t4 = C["tM"]

                      def oev_(e, oc=oc, tb=tb, par=par):
                          sl = XS[:, NT * oc + 512 * tb: NT * oc + 512 * (tb + 1)]
                          return e.tensor_add(sl, sl, p_o[par][:, :])
                      B.ev("vector", oev_, waits=(("tM", t4),), incs=(("vV", 1),))
                      oevd[par] = C["vV"]
                      it += 1
          B.barrier()


    if _KSTOP not in ("attn", "ffn1", "attnb", "qonly"):
        emit_ffn()

    for f in range(NF):
        B.ev("sync", dma_fn(d["y"][128 * f:128 * (f + 1), :], XS[:, NT * f:NT * (f + 1)]),
             incs=(("dS", 16),))
    B.ev("sync", None, waits=(("dS", C["dS"]),))

    with nc.Block() as blk:
        B.emit(blk)
    stack.close()
    return nc


_NC_CACHE = [None]


def kernel(**inputs):
    f32 = np.float32
    x = np.asarray(inputs["L0_attn_a_x"], f32)

    def pad32(W):
        Wp = np.zeros((1024, 512), f32)
        for h in range(16):
            Wp[:, 32 * h:32 * h + 16] = W[:, 16 * h:16 * h + 16]
        return Wp

    def pad32b(b):
        bp = np.zeros((1, 512), f32)
        for h in range(16):
            bp[0, 32 * h:32 * h + 16] = b[16 * h:16 * h + 16]
        return bp

    I = {k: np.asarray(v, f32) for k, v in inputs.items()}
    com = {
        "awq": pad32(I["L0_attn_a_Wq"]), "awqc": pad32(I["L0_attn_a_Wqc"]),
        "awk": pad32(I["L0_attn_a_Wk"]), "awkc": pad32(I["L0_attn_a_Wkc"]),
        "abq": pad32b(I["L0_attn_a_bq"]), "abqc": pad32b(I["L0_attn_a_bqc"]),
        "abk": pad32b(I["L0_attn_a_bk"]), "abkc": pad32b(I["L0_attn_a_bkc"]),
        "awv": np.ascontiguousarray(I["L0_attn_a_Wv"]),
        "awvc": np.ascontiguousarray(I["L0_attn_a_Wvc"]),
        "abv": I["L0_attn_a_bv"].reshape(1, 1024),
        "abvc": I["L0_attn_a_bvc"].reshape(1, 1024),
        "bwin": np.ascontiguousarray(I["L0_attn_b_W_in"].T),
        "bwout": np.ascontiguousarray(I["L0_attn_b_W_out"].T),
        "beig": I["L0_attn_b_eig_raw"].reshape(16, 1),
        "bgam": np.ascontiguousarray(I["L0_attn_b_ln_g"].reshape(8, 128).T),
        "bbet": np.ascontiguousarray(I["L0_attn_b_ln_b"].reshape(8, 128).T),
        "fwg": np.ascontiguousarray(I["L0_ffn_Wg"]),
        "fwu": np.ascontiguousarray(I["L0_ffn_Wu"]),
        "fwd": np.ascontiguousarray(
            I["L0_ffn_Wd"].reshape(8, 32, 128, 8, 128).transpose(0, 3, 1, 2, 4)),
    }
    wg = np.zeros((1024, 64), f32)
    wg[:, 0:16] = I["L0_attn_b_W_gate"][:16].T
    wg[:, 32:48] = I["L0_attn_b_W_gate"][16:].T
    bg = np.zeros((1, 64), f32)
    bg[0, 0:16] = I["L0_attn_b_b_gate"][:16]
    bg[0, 32:48] = I["L0_attn_b_b_gate"][16:]
    com["bwgate"] = wg
    com["bbgate"] = bg
    inda = np.zeros((16, 1024), f32)
    indb = np.zeros((48, 1024), f32)
    for f in range(8):
        inda[2 * f, 128 * f:128 * f + 64] = 1.0
        inda[2 * f + 1, 128 * f + 64:128 * f + 128] = 1.0
        indb[32 + 2 * f, 128 * f:128 * f + 64] = 1.0
        indb[32 + 2 * f + 1, 128 * f + 64:128 * f + 128] = 1.0
    com["inda"] = inda
    com["indb"] = indb

    if _NC_CACHE[0] is None:
        _NC_CACHE[0] = build_program()
    nc = _NC_CACHE[0]

    in_maps = []
    for b in range(8):
        m = dict(com)
        m["x"] = np.ascontiguousarray(x[b].T)
        in_maps.append(m)

    res = run_bass_kernel_spmd(nc, in_maps, core_ids=list(range(8)))
    out = np.stack([np.ascontiguousarray(r["y"].T) for r in res.results])
    return out.astype(np.float32)
